# version 03
# baseline (speedup 1.0000x reference)
import contextlib
import numpy as np
import concourse.bass as bass
import concourse.mybir as mybir
from concourse.bass_utils import run_bass_kernel_spmd

F32 = mybir.dt.float32
F16 = mybir.dt.float16
BF16 = mybir.dt.bfloat16
AF = mybir.ActivationFunctionType
ALU = mybir.AluOpType
AXX = mybir.AxisListType.X

H = 2048
DI = 4096
E2 = 2 * DI
N16 = 16
K4 = 4
R = 128
L = 2048
HF = 8192
NQ = 8
LQ = 256
T2 = 2
NKH = 16
NKD = 32
NKF = 64
EPS = 1e-5
DEBUG = False


class Sem:
    def __init__(self, handle):
        self.h = handle
        self.count = 0


class Prog:
    ENGS = ("sync", "vector", "scalar", "tensor", "gpsimd")

    def __init__(self):
        self.ops = {e: [] for e in self.ENGS}

    def emit(self, eng, builder, waits=(), inc=None):
        for s, v in waits:
            assert v <= s.count, f"wait {v} > scheduled {s.count} on {eng}"
        if inc is not None:
            inc[0].count += inc[1]
            incv = (inc[0].h, inc[1])
        else:
            incv = None
        self.ops[eng].append((list(waits), builder, incv))

    def replay(self, eng_name, eng):
        for waits, builder, incv in self.ops[eng_name]:
            for s, v in waits:
                if v > 0:
                    eng.wait_ge(s.h, v)
            ins = builder(eng)
            if incv is not None:
                ins.then_inc(incv[0], incv[1])


def build_nc():
    nc = bass.Bass("TRN2", target_bir_lowering=False, debug=False)

    dx = nc.dram_tensor("x", [L, H], F32, kind="ExternalInput").ap()
    dwinT = nc.dram_tensor("winT", [H, E2], F32, kind="ExternalInput").ap()
    dwconv = nc.dram_tensor("wconv2", [128, NKD * K4], F32, kind="ExternalInput").ap()
    dbconv = nc.dram_tensor("bconv2", [128, NKD], F32, kind="ExternalInput").ap()
    dwx = nc.dram_tensor("wx2", [128, NKD * 160], F32, kind="ExternalInput").ap()
    dwdtT = nc.dram_tensor("wdtT", [R, DI], F32, kind="ExternalInput").ap()
    dbdt = nc.dram_tensor("bdt2", [128, NKD], F32, kind="ExternalInput").ap()
    dwoutT = nc.dram_tensor("woutT", [DI, H], F32, kind="ExternalInput").ap()
    dalog = nc.dram_tensor("alog2", [128, NKD * N16], F32, kind="ExternalInput").ap()
    ddvec = nc.dram_tensor("dvec2", [128, NKD], F32, kind="ExternalInput").ap()
    dcfc = nc.dram_tensor("cfc", [H, HF], F32, kind="ExternalInput").ap()
    dcprojT = nc.dram_tensor("cprojT", [HF, H], F32, kind="ExternalInput").ap()
    deye = nc.dram_tensor("eye", [128, 128], F32, kind="ExternalInput").ap()
    dout = nc.dram_tensor("out", [L, H], F32, kind="ExternalOutput").ap()

    s_win_h = nc.dram_tensor("s_win_h", [H, E2], BF16).ap()
    s_win_l = nc.dram_tensor("s_win_l", [H, E2], BF16).ap()
    s_wout_h = nc.dram_tensor("s_wout_h", [DI, H], BF16).ap()
    s_wout_l = nc.dram_tensor("s_wout_l", [DI, H], BF16).ap()
    s_q1 = nc.dram_tensor("s_q1", [H, HF], BF16,
                          kind="ExternalOutput" if DEBUG else "Internal").ap()
    s_q2 = nc.dram_tensor("s_q2", [HF, H], BF16,
                          kind="ExternalOutput" if DEBUG else "Internal").ap()
    s_sg = nc.dram_tensor("s_sg", [DI, L], F32).ap()
    s_s = nc.dram_tensor("s_s", [LQ, HF], F32).ap()
    s_bc = nc.dram_tensor("s_bc", [32, LQ], F32).ap()
    s_sc = nc.dram_tensor("s_sc", [1, 8], F32).ap()

    taps = {}
    if DEBUG:
        taps["t_hs"] = nc.dram_tensor("t_hs", [128, 8192], F32, kind="ExternalOutput").ap()
        taps["t_dtin"] = nc.dram_tensor("t_dtin", [128, 2 * LQ], BF16, kind="ExternalOutput").ap()
        taps["t_y2"] = nc.dram_tensor("t_y2", [128, 16384], BF16, kind="ExternalOutput").ap()
        taps["t_mo"] = nc.dram_tensor("t_mo", [128, 4096], F32, kind="ExternalOutput").ap()
        taps["t_x2"] = nc.dram_tensor("t_x2", [128, 4096], F32, kind="ExternalOutput").ap()
        taps["t_rfm"] = nc.dram_tensor("t_rfm", [128, 16384], BF16, kind="ExternalOutput").ap()
        taps["t_hst"] = nc.dram_tensor("t_hst", [128, 512], F32, kind="ExternalOutput").ap()
        taps["t_sc"] = nc.dram_tensor("t_sc", [128, 8], F32, kind="ExternalOutput").ap()
        taps["t_st"] = nc.dram_tensor("t_st", [128, 32], F32, kind="ExternalOutput").ap()
        taps["t_o6"] = nc.dram_tensor("t_o6", [128, 4096], F32, kind="ExternalOutput").ap()

    A = nc.alloc_sbuf_tensor
    eye_s = A("eye_s", [128, 128], F32)
    wconv_c = A("wconv_c", [128, NKD * K4], F32)
    bconv_c = A("bconv_c", [128, NKD], F32)
    bdt_c = A("bdt_c", [128, NKD], F32)
    dvec_c = A("dvec_c", [128, NKD], F32)
    nA_c = A("nA_c", [128, NKD * N16], F32)
    sc_col = A("sc_col", [128, 8], F32)
    st = A("st", [128, 32], F32)
    accum = A("accum", [128, 96], F32)
    wx_hi = A("wx_hi", [128, NKD * 160], BF16)
    wx_lo = A("wx_lo", [128, NKD * 160], BF16)
    wdt_hi = A("wdt_hi", [128, DI], BF16)
    wdt_lo = A("wdt_lo", [128, DI], BF16)
    hstate = A("hstate", [128, NKD * N16], F32)
    hprev = A("hprev", [128, NKD * 3], F32)
    scr2k = A("scr2k", [128, H], F32)
    out_t = A("out_t", [128, H], F32)
    xq = A("xq", [128, T2 * H], F32)
    x1f = A("x1f", [128, 2 * NKH * LQ], BF16)
    WS = A("WS", [128, 16384], BF16)
    hs = A("hs", [128, NKD * LQ], F32)
    hsx = A("hsx", [128, LQ + 4], F32)
    hs_h = A("hs_h", [128, LQ], BF16)
    hs_l = A("hs_l", [128, LQ], BF16)
    cvt = A("cvt", [128, LQ], F32)
    sgt = A("sgt", [128, 2 * 128], F32)
    sg_c = A("sg_c", [128, 2 * LQ], F32)
    dtin_h = A("dtin_h", [128, LQ], BF16)
    dtin_l = A("dtin_l", [128, LQ], BF16)
    dt_t = A("dt_t", [128, LQ], F32)
    u_t = A("u_t", [128, LQ], F32)
    da_t = A("da_t", [128, 2 * LQ], F32)
    dbu_t = A("dbu_t", [128, 2 * LQ], F32)
    h_t = A("h_t", [128, 2 * LQ], F32)
    y_t = A("y_t", [128, LQ], F32)
    yt2 = A("yt2", [128, LQ], F32)
    pj_t = A("pj_t", [128, 512], F32)
    bc_t = A("bc_t", [128, LQ], F32)
    rg_t = A("rg_t", [128, 512], F32)
    sq_t = A("sq_t", [128, 2 * 512], F32)
    sload = A("sload", [128, 2 * 512], F32)
    v16 = A("v16", [128, 512], F16)
    x2 = A("x2", [128, T2 * H], F32)
    csts = A("csts", [128, 2], F32)

    def MO(a, b):
        return x1f[:, 2 * a:2 * b].bitcast(F32)

    def BB(a, b):
        return WS[:, 2 * a:2 * b].bitcast(F32)

    def BBS(n):
        if n < 8:
            return out_t[:, n * LQ:(n + 1) * LQ]
        if n < 12:
            return sq_t[:, (n - 8) * LQ:(n - 7) * LQ]
        return sload[:, (n - 12) * LQ:(n - 11) * LQ]

    def CBS(n):
        return x2[:, n * LQ:(n + 1) * LQ]

    def HSB(a, b):
        return hs[:, a // 2:b // 2].bitcast(BF16)

    def WSF16(a, b):
        return WS[:, a:b].bitcast(F16)

    P = nc.alloc_psum_tensor
    ps = [P(f"ps{i}", [128, 512], F32) for i in range(8)]

    prog = Prog()
    with contextlib.ExitStack() as stack:
        def mksem(name):
            return Sem(stack.enter_context(nc.semaphore(name)))

        def mkpool(name, n):
            return [mksem(f"{name}{i}") for i in range(n)]

        qs, qg, qc = mksem("q_sync"), mksem("q_gp"), mksem("q_sc")
        ev, es, et = mksem("e_v"), mksem("e_s"), mksem("e_t")
        PWH = mkpool("pwh", 16)
        PM6 = mkpool("pm6", 16)
        PWL = mkpool("pwl", 16)
        PP = mkpool("pp", 2)
        PPW = mkpool("ppw", 2)
        PXL = mkpool("pxl", 2)
        PBB = mkpool("pbb", 1)
        PBN = mkpool("pbn", 1)
        PSM = mkpool("psm", 1)
        PSGR = mkpool("psgr", 2)
        PSLR = mkpool("pslr", 2)
        PSPG = mkpool("pspg", 2)
        PSPQ = mkpool("pspq", 2)
        PST = mkpool("pst", 2)
        PTAP = mkpool("ptap", 1)
        block = stack.enter_context(nc.Block())

        DMAQ = {"sync": qs, "gpsimd": qg, "scalar": qc}
        EPROG = {"vector": ev, "scalar": es, "tensor": et}

        def dma(queue, out, in_, waits=(), sem=None):
            q = sem if sem is not None else DMAQ[queue]
            prog.emit(queue, lambda e, o=out, i=in_: e.dma_start(out=o, in_=i),
                      waits=waits, inc=(q, 16))
            return (q, q.count)

        def op(eng, builder, waits=()):
            p = EPROG[eng]
            prog.emit(eng, builder, waits=waits, inc=(p, 1))
            return (p, p.count)

        dma("sync", eye_s[:], deye[:, :], sem=PSM[0])
        dma("sync", wconv_c[:], dwconv[:, :], sem=PSM[0])
        dma("sync", bconv_c[:], dbconv[:, :], sem=PSM[0])
        dma("sync", bdt_c[:], dbdt[:, :], sem=PSM[0])
        dma("sync", dvec_c[:], ddvec[:, :], sem=PSM[0])
        m_small = dma("sync", nA_c[:], dalog[:, :], sem=PSM[0])
        m_eye = m_wc = m_bc = m_bd = m_dv = m_al = m_small
        m1 = op("scalar", lambda e: e.activation(out=nA_c[:], in_=nA_c[:],
                                                 func=AF.Exp), waits=[m_al])
        m_nA = op("vector", lambda e: e.tensor_scalar_mul(nA_c[:], nA_c[:], -1.0),
                  waits=[m1])
        m_hs0 = op("vector", lambda e: e.memset(hstate[:], 0.0))
        m_hp0 = op("vector", lambda e: e.memset(hprev[:], 0.0))
        op("vector", lambda e: e.memset(csts[:, 0:1], EPS))
        m_cst = op("vector", lambda e: e.memset(csts[:, 1:2], 1.0))

        m_wx = dma("sync", hs[:, 0:NKD * 160], dwx[:, :], sem=PP[0])
        m_wxh = op("scalar", lambda e: e.copy(wx_hi[:], hs[:, 0:NKD * 160]),
                   waits=[m_wx])
        m_wxl = op("vector", lambda e: e.tensor_tensor(
            wx_lo[:], hs[:, 0:NKD * 160], wx_hi[:], ALU.subtract), waits=[m_wxh])
        m_wdt = dma("sync", hs[:, 0:DI], dwdtT[:, :], waits=[m_wxh, m_wxl],
                    sem=PP[1])
        m_wdh = op("scalar", lambda e: e.copy(wdt_hi[:], hs[:, 0:DI]),
                   waits=[m_wdt])
        m_wdl = op("vector", lambda e: e.tensor_tensor(
            wdt_lo[:], hs[:, 0:DI], wdt_hi[:], ALU.subtract),
            waits=[m_wdh])

        last_v, last_s, last_wr = m_wxl, m_wdh, None

        def split_weight(dsrc, nk, ncols, dsth, dstl):
            nonlocal last_v, last_s, last_wr
            for k in range(nk):
                w = [last_v, last_s, m_wdl]
                if last_wr is not None:
                    w.append(last_wr)
                mld = dma("sync", hs[:, 0:ncols], dsrc[k * 128:(k + 1) * 128, :],
                          waits=w, sem=PP[k % 2])
                mh = op("scalar", lambda e, n_=ncols: e.copy(
                    WS[:, 0:n_], hs[:, 0:n_]), waits=[mld])
                ml = op("vector", lambda e, n_=ncols: e.tensor_tensor(
                    WS[:, 8192:8192 + n_], hs[:, 0:n_], WS[:, 0:n_],
                    ALU.subtract), waits=[mh])
                dma("gpsimd", dsth[k * 128:(k + 1) * 128, :], WS[:, 0:ncols],
                    waits=[mh], sem=PPW[0])
                mw2 = dma("gpsimd", dstl[k * 128:(k + 1) * 128, :],
                          WS[:, 8192:8192 + ncols], waits=[ml], sem=PPW[1])
                last_v, last_s, last_wr = ml, mh, mw2

        split_weight(dwinT, NKH, E2, s_win_h, s_win_l)
        split_weight(dwoutT, NKD, H, s_wout_h, s_wout_l)

        def ternary_prep(dsrc, nk, ncols, dstq, acc0, sc_i):
            nonlocal last_v, last_s, last_wr
            pass1 = []
            for k in range(nk):
                w = [last_v, last_s]
                if last_wr is not None:
                    w.append(last_wr)
                mld = dma("sync", hs[:, 0:ncols], dsrc[k * 128:(k + 1) * 128, :],
                          waits=w, sem=PP[k % 2])
                ma = op("scalar", lambda e, n_=ncols, k=k: e.activation(
                    out=BB(0, n_), in_=hs[:, 0:n_], func=AF.Abs,
                    accum_out=accum[:, acc0 + k:acc0 + k + 1]), waits=[mld])
                last_v, last_s, last_wr = ma, ma, None
                pass1.append(ma)
            mr = op("vector", lambda e: e.reduce_sum(
                accum[:, 88:89], accum[:, acc0:acc0 + nk], axis=AXX),
                waits=[pass1[-1]])
            mo_ = op("vector", lambda e: e.memset(accum[:, 89:90], 1.0))
            mps = op("tensor", lambda e: e.matmul(
                ps[6][0:1, 0:1], accum[:, 89:90], accum[:, 88:89],
                start=True, stop=True), waits=[mr, mo_])
            mv = op("vector", lambda e, nn=nk * 128 * ncols: e.tensor_scalar(
                accum[0:1, 90:91], ps[6][0:1, 0:1], 1.0 / nn, EPS,
                ALU.mult, ALU.max), waits=[mps])
            mb = dma("gpsimd", s_sc[0:1, sc_i:sc_i + 1], accum[0:1, 90:91],
                     waits=[mv], sem=PBN[0])
            mbc = dma("gpsimd", sc_col[:, sc_i:sc_i + 1],
                      s_sc[0:1, sc_i:sc_i + 1].partition_broadcast(128),
                      waits=[mb], sem=PBN[0])
            minv = op("vector", lambda e, i=sc_i: e.reciprocal(
                sc_col[:, i + 4:i + 5], sc_col[:, i:i + 1]), waits=[mbc])
            for k in range(nk):
                w = [last_v, last_s, minv]
                if last_wr is not None:
                    w.append(last_wr)
                mld = dma("sync", hs[:, 0:ncols], dsrc[k * 128:(k + 1) * 128, :],
                          waits=w, sem=PP[k % 2])
                mq0 = op("vector", lambda e, n_=ncols, i=sc_i: e.tensor_scalar(
                    hs[:, 0:n_], hs[:, 0:n_], sc_col[:, i + 4:i + 5], None,
                    ALU.mult), waits=[mld])
                mq1 = op("vector", lambda e, n_=ncols: e.tensor_scalar(
                    WSF16(0, n_), hs[:, 0:n_], 0.5, None, ALU.is_gt),
                    waits=[mq0])
                mq2 = op("vector", lambda e, n_=ncols: e.tensor_scalar(
                    hs[:, 0:n_], hs[:, 0:n_], -0.5, None, ALU.is_lt),
                    waits=[mq1])
                mq3 = op("vector", lambda e, n_=ncols: e.tensor_tensor(
                    WS[:, 8192:8192 + n_], WSF16(0, n_), hs[:, 0:n_],
                    ALU.subtract), waits=[mq2])
                mwq = dma("gpsimd", dstq[k * 128:(k + 1) * 128, :],
                          WS[:, 8192:8192 + ncols], waits=[mq3], sem=PPW[k % 2])
                last_v, last_s, last_wr = mq3, mq3, mwq
            return mbc

        mbc1 = ternary_prep(dcfc, NKH, HF, s_q1, 0, 0)
        mbc2 = ternary_prep(dcprojT, NKF, H, s_q2, 16, 1)
        prep_last = [(PPW[0], PPW[0].count), (PPW[1], PPW[1].count)]
        md1 = op("vector", lambda e: e.tensor_tensor(
            sc_col[:, 2:3], sc_col[:, 0:1], sc_col[:, 0:1], ALU.mult),
            waits=[mbc1])
        md2 = op("vector", lambda e: e.tensor_scalar(
            sc_col[:, 3:4], sc_col[:, 2:3], 255.0, None, ALU.mult), waits=[md1])
        md3 = op("vector", lambda e: e.tensor_scalar(
            sc_col[:, 6:7], sc_col[:, 1:2], 1.0 / 255.0, None, ALU.mult),
            waits=[mbc2])
        tapm = {}
        if DEBUG:
            tapm["sc"] = dma("gpsimd", taps["t_sc"][:, :], sc_col[:],
                             waits=[md1, md2, md3], sem=PTAP[0])

        def ln_stats(xsrc, si, extra_waits=()):
            w = list(extra_waits)
            ms1 = op("vector", lambda e: e.reduce_sum(st[:, si:si + 1], xsrc,
                                                      axis=AXX), waits=w)
            ms2 = op("scalar", lambda e: e.activation(
                out=out_t[:, 0:H], in_=xsrc, func=AF.Square,
                accum_out=st[:, si + 1:si + 2]), waits=w)
            ms3 = op("vector", lambda e: e.tensor_scalar_mul(
                st[:, si:si + 1], st[:, si:si + 1], 1.0 / H), waits=[ms1])
            ms4 = op("vector", lambda e: e.tensor_scalar_mul(
                st[:, si + 1:si + 2], st[:, si + 1:si + 2], 1.0 / H),
                waits=[ms2])
            ms5 = op("vector", lambda e: e.tensor_tensor(
                st[:, si + 2:si + 3], st[:, si:si + 1], st[:, si:si + 1],
                ALU.mult), waits=[ms3])
            ms6 = op("vector", lambda e: e.tensor_tensor(
                st[:, si + 1:si + 2], st[:, si + 1:si + 2],
                st[:, si + 2:si + 3], ALU.subtract), waits=[ms4, ms5])
            ms7 = op("scalar", lambda e: e.activation(
                out=st[:, si + 1:si + 2], in_=st[:, si + 1:si + 2],
                func=AF.Sqrt, bias=csts[:, 0:1]), waits=[ms6, m_cst])
            ms8 = op("vector", lambda e: e.reciprocal(
                st[:, si + 1:si + 2], st[:, si + 1:si + 2]), waits=[ms7])
            return ms8

        psum_free = {}

        def fm_split(t, waits_in):
            ml = None
            for hc in range(NKH):
                pb = ps[6] if hc % 2 == 0 else ps[7]
                tw = list(waits_in) + [m_eye]
                if psum_free.get(id(pb)) is not None:
                    tw.append(psum_free[id(pb)])
                mtr = op("tensor", lambda e, pb=pb, hc=hc: e.transpose(
                    pb[:, 0:128], scr2k[:, hc * 128:(hc + 1) * 128], eye_s[:]),
                    waits=tw)
                dh = x1f[:, hc * LQ + t * 128: hc * LQ + (t + 1) * 128]
                dl = x1f[:, 4096 + hc * LQ + t * 128: 4096 + hc * LQ + (t + 1) * 128]
                mh = op("scalar", lambda e, pb=pb, d=dh: e.activation(
                    out=d, in_=pb[:, 0:128], func=AF.Copy), waits=[mtr])
                ml = op("vector", lambda e, pb=pb, d=dl, s_=dh: e.tensor_tensor(
                    d, pb[:, 0:128], s_, ALU.subtract), waits=[mh])
                psum_free[id(pb)] = ml
            return ml

        xq_free = None
        hs_free = None
        ws_free = None
        x1_free = None
        sgt_free = {}
        sgc_free = {}
        sq_free = {}
        sload_free = {}
        out_dma_marks = [None, None]
        prev_store = None
        first_q = True

        for q in range(NQ):
            lo = q * LQ
            mx = []
            for t in range(T2):
                w = []
                if xq_free is not None:
                    w.append(xq_free)
                if prev_store is not None:
                    w.append(prev_store)
                mx.append(dma("gpsimd", xq[:, t * H:(t + 1) * H],
                              dx[lo + t * 128:lo + (t + 1) * 128, :], waits=w,
                              sem=PXL[t]))
            m_x1 = []
            prev_split = None
            for t in range(T2):
                xs = xq[:, t * H:(t + 1) * H]
                lw = [mx[t]]
                if out_dma_marks[t] is not None:
                    lw.append(out_dma_marks[t])
                ms = ln_stats(xs, 0, extra_waits=lw)
                maw = [ms]
                if prev_split is not None:
                    maw.append(prev_split)
                if x1_free is not None and t == 0:
                    maw.append(x1_free)
                ma = op("vector", lambda e, xs=xs: e.tensor_scalar(
                    scr2k[:], xs, st[:, 0:1], st[:, 1:2], ALU.subtract,
                    ALU.mult), waits=maw)
                prev_split = fm_split(t, [ma])
                m_x1.append(prev_split)

            ws_slot = {}
            conv_marks = {}
            gate_spill = {}
            hs_piece = {}
            m2_last = None
            hs_gate_done = None
            last_m1_mm = None

            def m1_eblock(eb):
                nonlocal last_m1_mm
                mm_last = {}
                for k in range(NKH):
                    so = k * 1024
                    w = []
                    if first_q and eb == 0 and k == 0:
                        w += prep_last
                    if eb == 0 and k == 0 and ws_free is not None:
                        w.append(ws_free)
                    if ws_slot.get(k) is not None:
                        w.append(ws_slot[k])
                    mh_l = dma("sync", WS[:, so:so + 512],
                               s_win_h[k * 128:(k + 1) * 128,
                                       eb * 512:(eb + 1) * 512], waits=w,
                               sem=PWH[k])
                    ml_l = dma("scalar", WS[:, so + 512:so + 1024],
                               s_win_l[k * 128:(k + 1) * 128,
                                       eb * 512:(eb + 1) * 512], waits=w,
                               sem=PWL[k])
                    for t in range(T2):
                        pb = ps[t]
                        pw = [mh_l, ml_l, m_x1[t]]
                        if k == 0 and psum_free.get(id(pb)) is not None:
                            pw.append(psum_free[id(pb)])
                        xh = x1f[:, k * LQ + t * 128: k * LQ + (t + 1) * 128]
                        xl = x1f[:, 4096 + k * LQ + t * 128:
                                 4096 + k * LQ + (t + 1) * 128]
                        wh = WS[:, so:so + 512]
                        wl = WS[:, so + 512:so + 1024]
                        op("tensor", lambda e, pb=pb, xh=xh, wh=wh, s_=(k == 0):
                           e.matmul(pb[:, 0:512], xh, wh, start=s_, stop=False),
                           waits=pw if k == 0 else [mh_l, ml_l])
                        op("tensor", lambda e, pb=pb, xh=xh, wl=wl:
                           e.matmul(pb[:, 0:512], xh, wl, start=False,
                                    stop=False))
                        mm = op("tensor", lambda e, pb=pb, xl=xl, wh=wh,
                                p_=(k == NKH - 1):
                                e.matmul(pb[:, 0:512], xl, wh, start=False,
                                         stop=p_))
                        mm_last[t] = mm
                    ws_slot[k] = mm_last[T2 - 1]
                    last_m1_mm = mm_last[T2 - 1]
                return mm_last

            def m1_evac(eb, t, mm_t, prev_tr):
                ew_ = [mm_t]
                if prev_tr is not None:
                    ew_.append(prev_tr)
                me_ = op("scalar", lambda e, pb=ps[t]: e.activation(
                    out=pj_t[:], in_=pb[:, 0:512], func=AF.Copy), waits=ew_)
                psum_free[id(ps[t])] = me_
                last_tr = None
                for j in range(4):
                    c = eb * 4 + j
                    pb2 = ps[6] if j % 2 == 0 else ps[7]
                    tw = [me_]
                    if psum_free.get(id(pb2)) is not None:
                        tw.append(psum_free[id(pb2)])
                    mtr = op("tensor", lambda e, pb2=pb2, j=j: e.transpose(
                        pb2[:, 0:128], pj_t[:, j * 128:(j + 1) * 128], eye_s[:]),
                        waits=tw)
                    last_tr = mtr
                    if c < NKD:
                        hw = [mtr]
                        if hs_free is not None and eb == 0 and j == 0 and t == 0:
                            hw.append(hs_free)
                            if tapm.get("rfm") is not None:
                                hw.append(tapm["rfm"])
                        mev = op("vector", lambda e, pb2=pb2, c=c, t=t:
                                 e.tensor_copy(hs[:, c * LQ + t * 128:
                                               c * LQ + (t + 1) * 128],
                                               pb2[:, 0:128]), waits=hw)
                        hs_piece.setdefault(c, []).append(mev)
                    else:
                        g = c - NKD
                        sb = (g * 2 + t) % 2
                        gw = [mtr]
                        if sgt_free.get(sb) is not None:
                            gw.append(sgt_free[sb])
                        mev = op("scalar", lambda e, pb2=pb2, sb=sb: e.activation(
                            out=sgt[:, sb * 128:(sb + 1) * 128],
                            in_=pb2[:, 0:128], func=AF.Silu), waits=gw)
                        sgt_free[sb] = dma(
                            "gpsimd",
                            s_sg[g * 128:(g + 1) * 128,
                                 lo + t * 128:lo + (t + 1) * 128],
                            sgt[:, sb * 128:(sb + 1) * 128], waits=[mev],
                            sem=PSPG[sb])
                        gate_spill[(g, t)] = sgt_free[sb]
                    psum_free[id(pb2)] = mev
                return last_tr

            prev_tr = None
            for eb in range(8):
                mm_last = m1_eblock(eb)
                for t in range(T2):
                    prev_tr = m1_evac(eb, t, mm_last[t], prev_tr)
                for j in range(4):
                    c = eb * 4 + j
                    pw = list(hs_piece[c])
                    mh1 = op("vector", lambda e, c=c: e.tensor_copy(
                        hsx[:, 0:3], hprev[:, c * 3:(c + 1) * 3]),
                        waits=pw + [m_hp0])
                    mh2 = op("vector", lambda e, c=c: e.tensor_copy(
                        hsx[:, 3:LQ + 3], hs[:, c * LQ:(c + 1) * LQ]), waits=pw)
                    mh3 = op("vector", lambda e, c=c: e.tensor_copy(
                        hprev[:, c * 3:(c + 1) * 3], hsx[:, LQ:LQ + 3]),
                        waits=[mh2])
                    mc = op("vector", lambda e, c=c: e.tensor_scalar(
                        cvt[:], hsx[:, 0:LQ], wconv_c[:, c * K4:c * K4 + 1],
                        None, ALU.mult), waits=[mh1, mh2, m_wc])
                    for kk in range(1, K4):
                        mc = op("vector", lambda e, c=c, kk=kk:
                                e.scalar_tensor_tensor(
                                    cvt[:], hsx[:, kk:kk + LQ],
                                    wconv_c[:, c * K4 + kk:c * K4 + kk + 1],
                                    cvt[:], ALU.mult, ALU.add), waits=[mc])
                    mhs = op("scalar", lambda e, c=c: e.activation(
                        out=hs[:, c * LQ:(c + 1) * LQ], in_=cvt[:],
                        func=AF.Silu, bias=bconv_c[:, c:c + 1]),
                        waits=[mc, m_bc, mh3])
                    conv_marks[c] = mhs
                    hhw = [mhs]
                    if m2_last is not None:
                        hhw.append(m2_last)
                    mhh = op("scalar", lambda e, c=c: e.copy(
                        hs_h[:], hs[:, c * LQ:(c + 1) * LQ]), waits=hhw)
                    mhl = op("vector", lambda e, c=c: e.tensor_tensor(
                        hs_l[:], hs[:, c * LQ:(c + 1) * LQ], hs_h[:],
                        ALU.subtract), waits=[mhh])
                    m2w = [mhh, mhl, m_wxl]
                    if c == 0:
                        for b in (2, 3):
                            if psum_free.get(id(ps[b])) is not None:
                                m2w.append(psum_free[id(ps[b])])
                    for pi, (wt, xt) in enumerate(
                            [(wx_hi, hs_h), (wx_hi, hs_l), (wx_lo, hs_h)]):
                        s_ = (c == 0 and pi == 0)
                        p_ = (c == NKD - 1 and pi == 2)
                        op("tensor", lambda e, w=wt[:, c * 160:c * 160 + 128],
                           xt=xt, s_=s_, p_=p_:
                           e.matmul(ps[2][:, 0:LQ], w, xt[:], start=s_,
                                    stop=p_), waits=m2w if pi == 0 else [])
                        mmb = op("tensor",
                                 lambda e, w=wt[:, c * 160 + 128:(c + 1) * 160],
                                 xt=xt, s_=s_, p_=p_:
                                 e.matmul(ps[3][0:32, 0:LQ], w, xt[:],
                                          start=s_, stop=p_))
                    m2_last = mmb

            for eb in range(8, 16):
                mm_last = m1_eblock(eb)
                for t in range(T2):
                    prev_tr = m1_evac(eb, t, mm_last[t], prev_tr)
            spg_fin = (sgt_free[0], sgt_free[1])

            mdh = op("scalar", lambda e: e.activation(
                out=dtin_h[:], in_=ps[2][:, 0:LQ], func=AF.Copy),
                waits=[m2_last])
            mdl = op("vector", lambda e: e.tensor_tensor(
                dtin_l[:], ps[2][:, 0:LQ], dtin_h[:], ALU.subtract),
                waits=[mdh])
            psum_free[id(ps[2])] = mdl
            mbce = op("vector", lambda e: e.tensor_copy(
                bc_t[0:32, 0:LQ], ps[3][0:32, 0:LQ]), waits=[m2_last])
            psum_free[id(ps[3])] = mbce
            mbcd = dma("gpsimd", s_bc[:, :], bc_t[0:32, 0:LQ], waits=[mbce],
                       sem=PBN[0])
            bb_w = [mbcd]
            for t_ in range(T2):
                if out_dma_marks[t_] is not None:
                    bb_w.append(out_dma_marks[t_])
            for sb_ in (0, 1):
                if sq_free.get(sb_) is not None:
                    bb_w.append(sq_free[sb_])
                if sload_free.get(sb_) is not None:
                    bb_w.append(sload_free[sb_])
            if prev_store is not None:
                bb_w.append(prev_store)
            m_bb = None
            for n in range(N16):
                dma("gpsimd", BBS(n),
                    s_bc[n:n + 1, :].partition_broadcast(128), waits=bb_w,
                    sem=PBB[0])
                m_bb = dma("gpsimd", CBS(n),
                           s_bc[N16 + n:N16 + n + 1, :].partition_broadcast(128),
                           waits=[mbcd], sem=PBB[0])

            if DEBUG and q == 0:
                tapm["hs"] = dma("gpsimd", taps["t_hs"][:, :], hs[:, 0:8192],
                                 waits=[conv_marks[c] for c in range(NKD)],
                                 sem=PTAP[0])
                dma("gpsimd", taps["t_dtin"][:, 0:LQ], dtin_h[:], waits=[mdl],
                    sem=PTAP[0])
                tapm["dtin"] = dma("gpsimd", taps["t_dtin"][:, LQ:2 * LQ],
                                   dtin_l[:], waits=[mdl], sem=PTAP[0])

            y2_marks = []
            scan_last = None
            prev_mu = None
            prev_m2h = None
            for c in range(NKD):
                pb = ps[4 + c % 2]
                m3w = [mdh, mdl, m_wdl]
                if psum_free.get(id(pb)) is not None:
                    m3w.append(psum_free[id(pb)])
                mm3 = None
                for pi, (wt_, xt_) in enumerate([(wdt_hi, dtin_h),
                                                 (wdt_hi, dtin_l),
                                                 (wdt_lo, dtin_h)]):
                    mm3 = op("tensor", lambda e, pb=pb,
                             w=wt_[:, c * 128:(c + 1) * 128], x=xt_,
                             s_=(pi == 0), p_=(pi == 2):
                             e.matmul(pb[:, 0:LQ], w, x[:], start=s_, stop=p_),
                             waits=m3w if pi == 0 else [])
                s1w = [mm3, m_bd]
                if prev_mu is not None:
                    s1w.append(prev_mu)
                ms1 = op("scalar", lambda e, pb=pb, c=c: e.activation(
                    out=dt_t[:], in_=pb[:, 0:LQ], func=AF.Exp,
                    bias=bdt_c[:, c:c + 1]), waits=s1w)
                psum_free[id(pb)] = ms1
                ms2 = op("scalar", lambda e: e.activation(
                    out=dt_t[:], in_=dt_t[:], func=AF.Ln, bias=csts[:, 1:2]),
                    waits=[ms1, m_cst])
                mu = op("vector", lambda e, c=c: e.tensor_tensor(
                    u_t[:], dt_t[:], hs[:, c * LQ:(c + 1) * LQ], ALU.mult),
                    waits=[ms2, conv_marks[c]])
                prev_mu = mu
                sb = c % 2
                gw = [spg_fin[0], spg_fin[1]]
                if sgc_free.get(sb) is not None:
                    gw.append(sgc_free[sb])
                msg = dma("gpsimd", sg_c[:, sb * LQ:(sb + 1) * LQ],
                          s_sg[c * 128:(c + 1) * 128, lo:lo + LQ], waits=gw,
                          sem=PSGR[sb])
                my = None
                prev_y = {}
                for n in range(N16):
                    p2 = n % 2
                    daw = [ms2, m_nA]
                    if prev_y.get(p2) is not None:
                        daw.append(prev_y[p2])
                    yt2w = [prev_m2h] if (n == 1 and prev_m2h is not None) else []
                    mda = op("scalar", lambda e, c=c, n=n, p2=p2: e.activation(
                        out=da_t[:, p2 * LQ:(p2 + 1) * LQ], in_=dt_t[:],
                        func=AF.Exp,
                        scale=nA_c[:, c * N16 + n:c * N16 + n + 1]), waits=daw)
                    mdb = op("vector", lambda e, n=n, p2=p2: e.tensor_tensor(
                        dbu_t[:, p2 * LQ:(p2 + 1) * LQ], u_t[:],
                        BBS(n), ALU.mult),
                        waits=[mu, m_bb])
                    msc = op("vector", lambda e, c=c, n=n, p2=p2:
                             e.tensor_tensor_scan(
                                 h_t[:, p2 * LQ:(p2 + 1) * LQ],
                                 da_t[:, p2 * LQ:(p2 + 1) * LQ],
                                 dbu_t[:, p2 * LQ:(p2 + 1) * LQ],
                                 hstate[:, c * N16 + n:c * N16 + n + 1],
                                 ALU.mult, ALU.add), waits=[mda, mdb, m_hs0])
                    op("vector", lambda e, c=c, n=n, p2=p2: e.tensor_copy(
                        hstate[:, c * N16 + n:c * N16 + n + 1],
                        h_t[:, p2 * LQ + LQ - 1:p2 * LQ + LQ]), waits=[msc])
                    if n == 0:
                        my = op("vector", lambda e, n=n, p2=p2: e.tensor_tensor(
                            y_t[:], h_t[:, p2 * LQ:(p2 + 1) * LQ],
                            CBS(n), ALU.mult),
                            waits=[msc])
                    else:
                        mt_ = op("vector", lambda e, n=n, p2=p2: e.tensor_tensor(
                            yt2[:], h_t[:, p2 * LQ:(p2 + 1) * LQ],
                            CBS(n), ALU.mult),
                            waits=[msc] + yt2w)
                        my = op("vector", lambda e: e.tensor_tensor(
                            y_t[:], y_t[:], yt2[:], ALU.add), waits=[mt_])
                    prev_y[p2] = my
                my2 = op("vector", lambda e, c=c: e.scalar_tensor_tensor(
                    y_t[:], hs[:, c * LQ:(c + 1) * LQ], dvec_c[:, c:c + 1],
                    y_t[:], ALU.mult, ALU.add), waits=[my, m_dv])
                my3 = op("vector", lambda e, sb=sb: e.tensor_tensor(
                    yt2[:], y_t[:], sg_c[:, sb * LQ:(sb + 1) * LQ], ALU.mult),
                    waits=[my2, msg])
                sgc_free[sb] = my3
                h2w = [my3]
                if c == 0 and tapm.get("hs") is not None:
                    h2w.append(tapm["hs"])
                m2h = op("scalar", lambda e, c=c: e.copy(
                    HSB(c * 2 * LQ, c * 2 * LQ + LQ), yt2[:]), waits=h2w)
                prev_m2h = m2h
                m2l = op("vector", lambda e, c=c: e.tensor_tensor(
                    HSB(c * 2 * LQ + LQ, (c + 1) * 2 * LQ), yt2[:],
                    HSB(c * 2 * LQ, c * 2 * LQ + LQ), ALU.subtract),
                    waits=[m2h])
                y2_marks.append(m2l)
                scan_last = m2l

            if DEBUG and q == 0:
                tapm["y2"] = dma("gpsimd", taps["t_y2"][:, :], HSB(0, 16384),
                                 waits=[scan_last], sem=PTAP[0])
                tapm["hst"] = dma("gpsimd", taps["t_hst"][:, :], hstate[:],
                                  waits=[scan_last], sem=PTAP[0])

            ws_slot_c = {}
            mo_done = [[], []]
            m4_last_mm = None
            for hb in range(4):
                mm_last = {}
                for k in range(NKD):
                    slot = k % 8
                    so = slot * 2048
                    w = [y2_marks[k]]
                    if first_q and hb == 0 and k == 0:
                        w.extend(prep_last)
                    if hb == 0 and k < 8:
                        w.append(ws_slot[2 * (k % 8)])
                    if ws_slot_c.get(slot) is not None:
                        w.append(ws_slot_c[slot])
                    mh_l = dma("sync", WS[:, so:so + 512],
                               s_wout_h[k * 128:(k + 1) * 128,
                                        hb * 512:(hb + 1) * 512], waits=w,
                               sem=PWH[k % 16])
                    ml_l = dma("scalar", WS[:, so + 512:so + 1024],
                               s_wout_l[k * 128:(k + 1) * 128,
                                        hb * 512:(hb + 1) * 512], waits=w,
                               sem=PWL[k % 16])
                    for t in range(T2):
                        pb = ps[t]
                        pw = [mh_l, ml_l]
                        if k == 0 and psum_free.get(id(pb)) is not None:
                            pw.append(psum_free[id(pb)])
                        yh = HSB(k * 2 * LQ + t * 128, k * 2 * LQ + (t + 1) * 128)
                        yl = HSB(k * 2 * LQ + LQ + t * 128,
                                 k * 2 * LQ + LQ + (t + 1) * 128)
                        wh = WS[:, so:so + 512]
                        wl = WS[:, so + 512:so + 1024]
                        op("tensor", lambda e, pb=pb, yh=yh, wh=wh, s_=(k == 0):
                           e.matmul(pb[:, 0:512], yh, wh, start=s_, stop=False),
                           waits=pw if k == 0 else [mh_l, ml_l])
                        op("tensor", lambda e, pb=pb, yh=yh, wl=wl:
                           e.matmul(pb[:, 0:512], yh, wl, start=False,
                                    stop=False))
                        mm = op("tensor", lambda e, pb=pb, yl=yl, wh=wh,
                                p_=(k == NKD - 1):
                                e.matmul(pb[:, 0:512], yl, wh, start=False,
                                         stop=p_))
                        mm_last[t] = mm
                    ws_slot_c[slot] = mm_last[T2 - 1]
                m4_last_mm = mm_last[T2 - 1]
                for t in range(T2):
                    pb = ps[t]
                    ew = [mm_last[t]]
                    if hb == 0 and t == 0:
                        ew.append(m_x1[1])
                        ew.append(last_m1_mm)
                    mev = op("vector", lambda e, pb=pb, t=t, hb=hb: e.tensor_copy(
                        MO(t * H + hb * 512, t * H + (hb + 1) * 512),
                        pb[:, 0:512]), waits=ew)
                    psum_free[id(pb)] = mev
                    mo_done[t].append(mev)
            if DEBUG and q == 0:
                tapm["mo"] = dma("gpsimd", taps["t_mo"][:, :], MO(0, 4096),
                                 waits=mo_done[0] + mo_done[1], sem=PTAP[0])
            x2_done = []
            for t in range(T2):
                xs = MO(t * H, (t + 1) * H)
                ms = ln_stats(xs, 4, extra_waits=list(mo_done[t]))
                ma = op("vector", lambda e, xs=xs: e.tensor_scalar(
                    scr2k[:], xs, st[:, 4:5], st[:, 5:6], ALU.subtract,
                    ALU.mult), waits=[ms])
                mr_ = op("vector", lambda e, t=t: e.tensor_tensor(
                    x2[:, t * H:(t + 1) * H], scr2k[:],
                    xq[:, t * H:(t + 1) * H], ALU.add), waits=[ma, mx[t]])
                x2_done.append(mr_)
            xq_free = x2_done[-1]

            if DEBUG and q == 0:
                tapm["x2"] = dma("gpsimd", taps["t_x2"][:, :], x2[:],
                                 waits=x2_done, sem=PTAP[0])

            m_x3 = []
            prev_split = None
            for t in range(T2):
                xs = x2[:, t * H:(t + 1) * H]
                ms = ln_stats(xs, 8, extra_waits=[x2_done[t]])
                maw = [ms]
                if t == 0:
                    maw.append(x2_done[1])
                    if tapm.get("mo") is not None:
                        maw.append(tapm["mo"])
                if prev_split is not None:
                    maw.append(prev_split)
                ma = op("vector", lambda e, xs=xs: e.tensor_scalar(
                    scr2k[:], xs, st[:, 8:9], st[:, 9:10], ALU.subtract,
                    ALU.mult), waits=maw)
                prev_split = fm_split(t, [ma])
                m_x3.append(prev_split)

            gmax_init = [op("vector", lambda e, t=t: e.memset(
                st[:, 12 + t:13 + t], 0.0)) for t in range(T2)]
            gmax_last = [gmax_init[0], gmax_init[1]]
            ws_slot_d = {}
            sq_spills = {}
            m5_last_mm = None
            prev_mrx = None
            for fb in range(16):
                mm_last = {}
                for k in range(NKH):
                    so = k * 512
                    w = [m_x3[0], m_x3[1]]
                    if first_q and fb == 0 and k == 0:
                        pass
                    if fb == 0:
                        w.append(m4_last_mm)
                    if ws_slot_d.get(k) is not None:
                        w.append(ws_slot_d[k])
                    mq_l = dma("sync", WS[:, so:so + 512],
                               s_q1[k * 128:(k + 1) * 128,
                                    fb * 512:(fb + 1) * 512], waits=w,
                               sem=PWH[k])
                    for t in range(T2):
                        pb = ps[2 + t]
                        pw = [mq_l]
                        if k == 0 and psum_free.get(id(pb)) is not None:
                            pw.append(psum_free[id(pb)])
                        xh = x1f[:, k * LQ + t * 128: k * LQ + (t + 1) * 128]
                        xl = x1f[:, 4096 + k * LQ + t * 128:
                                 4096 + k * LQ + (t + 1) * 128]
                        wq = WS[:, so:so + 512]
                        op("tensor", lambda e, pb=pb, xh=xh, wq=wq, s_=(k == 0):
                           e.matmul(pb[:, 0:512], xh, wq, start=s_, stop=False),
                           waits=pw if k == 0 else [mq_l])
                        mm = op("tensor", lambda e, pb=pb, xl=xl, wq=wq,
                                p_=(k == NKH - 1):
                                e.matmul(pb[:, 0:512], xl, wq, start=False,
                                         stop=p_))
                        mm_last[t] = mm
                    ws_slot_d[k] = mm_last[T2 - 1]
                m5_last_mm = mm_last[T2 - 1]
                for t in range(T2):
                    pb = ps[2 + t]
                    rlw = [mm_last[t]]
                    if prev_mrx is not None:
                        rlw.append(prev_mrx)
                    mrl = op("scalar", lambda e, pb=pb: e.activation(
                        out=rg_t[:], in_=pb[:, 0:512], func=AF.Relu),
                        waits=rlw)
                    psum_free[id(pb)] = mrl
                    mrx = op("vector", lambda e: e.reduce_max(
                        st[:, 16:17], rg_t[:], axis=AXX), waits=[mrl])
                    prev_mrx = mrx
                    mgm = op("vector", lambda e, t=t: e.tensor_tensor(
                        st[:, 12 + t:13 + t], st[:, 12 + t:13 + t],
                        st[:, 16:17], ALU.max), waits=[mrx, gmax_last[t]])
                    gmax_last[t] = mgm
                    sb = (fb * 2 + t) % 2
                    sqw = [mrl, mrx]
                    if sq_free.get(sb) is not None:
                        sqw.append(sq_free[sb])
                    msq = op("scalar", lambda e, sb=sb: e.activation(
                        out=sq_t[:, sb * 512:(sb + 1) * 512], in_=rg_t[:],
                        func=AF.Square), waits=sqw)
                    sq_free[sb] = dma("gpsimd",
                                      s_s[t * 128:(t + 1) * 128,
                                          fb * 512:(fb + 1) * 512],
                                      sq_t[:, sb * 512:(sb + 1) * 512],
                                      waits=[msq], sem=PSPQ[sb])
                    sq_spills[(fb, t)] = sq_free[sb]

            spq_fin = (sq_free[0], sq_free[1])

            fso = []
            for t in range(T2):
                w1 = op("vector", lambda e, t=t: e.tensor_tensor(
                    st[:, 16 + t:17 + t], st[:, 12 + t:13 + t],
                    st[:, 12 + t:13 + t], ALU.mult), waits=[gmax_last[t]])
                w2 = op("vector", lambda e, t=t: e.tensor_scalar(
                    st[:, 16 + t:17 + t], st[:, 16 + t:17 + t],
                    sc_col[:, 2:3], EPS, ALU.mult, ALU.max),
                    waits=[w1, md2])
                w3 = op("vector", lambda e, t=t: e.reciprocal(
                    st[:, 18 + t:19 + t], st[:, 16 + t:17 + t]), waits=[w2])
                w4 = op("vector", lambda e, t=t: e.tensor_tensor(
                    st[:, 18 + t:19 + t], st[:, 18 + t:19 + t],
                    sc_col[:, 3:4], ALU.mult), waits=[w3])
                w5 = op("vector", lambda e, t=t: e.tensor_tensor(
                    st[:, 20 + t:21 + t], st[:, 16 + t:17 + t],
                    sc_col[:, 6:7], ALU.mult), waits=[w2, md3])
                fso.append((w4, w5))

            if DEBUG and q == 0:
                tapm["st"] = dma("gpsimd", taps["t_st"][:, :], st[:],
                                 waits=[fso[0][0], fso[0][1], fso[1][0],
                                        fso[1][1]], sem=PTAP[0])

            r_fm_marks = {}
            rr_last = None
            prev_rtr = None
            first_rfm = True
            for fb in range(16):
                for t in range(T2):
                    sb = (fb * 2 + t) % 2
                    rw = [spq_fin[0], spq_fin[1], fso[t][0]]
                    if sload_free.get(sb) is not None:
                        rw.append(sload_free[sb])
                    mld = dma("gpsimd", sload[:, sb * 512:(sb + 1) * 512],
                              s_s[t * 128:(t + 1) * 128,
                                  fb * 512:(fb + 1) * 512], waits=rw,
                              sem=PSLR[sb])
                    mv1 = op("vector", lambda e, sb=sb, t=t: e.tensor_scalar(
                        v16[:], sload[:, sb * 512:(sb + 1) * 512],
                        st[:, 18 + t:19 + t], 1024.0, ALU.mult, ALU.add),
                        waits=[mld])
                    sload_free[sb] = mv1
                    vw = [mv1]
                    if prev_rtr is not None:
                        vw.append(prev_rtr)
                    mv2 = op("vector", lambda e: e.tensor_scalar(
                        pj_t[:], v16[:], -1024.0, None, ALU.add), waits=vw)
                    for j in range(4):
                        rc = fb * 4 + j
                        pb2 = ps[6] if j % 2 == 0 else ps[7]
                        tw = [mv2]
                        if psum_free.get(id(pb2)) is not None:
                            tw.append(psum_free[id(pb2)])
                        mtr = op("tensor", lambda e, pb2=pb2, j=j: e.transpose(
                            pb2[:, 0:128], pj_t[:, j * 128:(j + 1) * 128],
                            eye_s[:]), waits=tw)
                        prev_rtr = mtr
                        hw = [mtr]
                        if first_rfm:
                            hw.append(m4_last_mm)
                            if tapm.get("y2") is not None:
                                hw.append(tapm["y2"])
                            first_rfm = False
                        mev = op("scalar", lambda e, pb2=pb2, rc=rc, t=t:
                                 e.activation(
                                     out=HSB(rc * LQ + t * 128,
                                             rc * LQ + (t + 1) * 128),
                                     in_=pb2[:, 0:128], func=AF.Copy),
                                 waits=hw)
                        psum_free[id(pb2)] = mev
                        r_fm_marks.setdefault(rc, []).append(mev)
                        rr_last = mev

            if DEBUG and q == 0:
                tapm["rfm"] = dma("gpsimd", taps["t_rfm"][:, :], HSB(0, 16384),
                                  waits=[rr_last], sem=PTAP[0])

            ws_slot_e = {}
            o6_done = [[], []]
            for db in range(4):
                mm_last = {}
                for k in range(NKF):
                    slot = k % 16
                    so = 8192 + slot * 512
                    w = []
                    if first_q and db == 0 and k == 0:
                        pass
                    if db == 0 and k < 16:
                        w.append(m4_last_mm)
                    if ws_slot_e.get(slot) is not None:
                        w.append(ws_slot_e[slot])
                    mq_l = dma("sync", WS[:, so:so + 512],
                               s_q2[k * 128:(k + 1) * 128,
                                    db * 512:(db + 1) * 512], waits=w,
                               sem=PM6[k % 16])
                    for t in range(T2):
                        pb = ps[t]
                        pw = [mq_l, r_fm_marks[k][-1]]
                        if k == 0 and psum_free.get(id(pb)) is not None:
                            pw.append(psum_free[id(pb)])
                        rf = HSB(k * LQ + t * 128, k * LQ + (t + 1) * 128)
                        wq = WS[:, so:so + 512]
                        mm = op("tensor", lambda e, pb=pb, rf=rf, wq=wq,
                                s_=(k == 0), p_=(k == NKF - 1):
                                e.matmul(pb[:, 0:512], rf, wq, start=s_,
                                         stop=p_),
                                waits=pw if (db == 0 or k == 0) else [mq_l])
                        mm_last[t] = mm
                    ws_slot_e[slot] = mm_last[T2 - 1]
                for t in range(T2):
                    pb = ps[t]
                    ew = [mm_last[t], fso[t][1]]
                    if db == 0:
                        ew.append(xq_free)
                    mev = op("vector", lambda e, pb=pb, t=t, db=db:
                             e.tensor_scalar(
                                 xq[:, t * H + db * 512:
                                    t * H + (db + 1) * 512],
                                 pb[:, 0:512], st[:, 20 + t:21 + t], None,
                                 ALU.mult), waits=ew)
                    psum_free[id(pb)] = mev
                    o6_done[t].append(mev)
            m6_last_mm = mm_last[T2 - 1]

            if DEBUG and q == 0:
                tapm["o6"] = dma("gpsimd", taps["t_o6"][:, :], xq[:],
                                 waits=o6_done[0] + o6_done[1], sem=PTAP[0])

            store_marks = []
            for t in range(T2):
                xs = xq[:, t * H:(t + 1) * H]
                lw = list(o6_done[t])
                if t == 1:
                    lw.append(store_marks[0])
                ms = ln_stats(xs, 24, extra_waits=lw)
                ma = op("vector", lambda e, xs=xs: e.tensor_scalar(
                    scr2k[:], xs, st[:, 24:25], st[:, 25:26], ALU.subtract,
                    ALU.mult), waits=[ms])
                mw_ = [ma]
                if prev_store is not None and t == 0:
                    mw_.append(prev_store)
                mr_ = op("vector", lambda e, t=t: e.tensor_tensor(
                    out_t[:, 0:H], scr2k[:], x2[:, t * H:(t + 1) * H],
                    ALU.add), waits=mw_)
                mst = dma("gpsimd", dout[lo + t * 128:lo + (t + 1) * 128, :],
                          out_t[:, 0:H], waits=[mr_], sem=PST[t])
                store_marks.append(mst)
            prev_store = store_marks[-1]
            out_dma_marks = list(store_marks)
            ws_free = m6_last_mm
            hs_free = m6_last_mm
            x1_free = m5_last_mm
            first_q = False

        prog.emit("vector", lambda e: e.engine_nop(),
                  waits=list(out_dma_marks))

        @block.sync
        def _(sync):
            prog.replay("sync", sync)

        @block.vector
        def _(vector):
            prog.replay("vector", vector)

        @block.scalar
        def _(scalar):
            prog.replay("scalar", scalar)

        @block.tensor
        def _(tensor):
            prog.replay("tensor", tensor)

        @block.gpsimd
        def _(gpsimd):
            prog.replay("gpsimd", gpsimd)

    return nc


def kernel(L0_attn_x, L0_attn_Win, L0_attn_Wconv, L0_attn_bconv, L0_attn_Wx,
           L0_attn_Wdt, L0_attn_bdt, L0_attn_Wout, L0_attn_A_log, L0_attn_D,
           L0_ffn_c_fc, L0_ffn_c_proj):
    x = np.ascontiguousarray(np.asarray(L0_attn_x, dtype=np.float32))
    winT = np.ascontiguousarray(np.asarray(L0_attn_Win, np.float32).T)
    wconv2 = np.ascontiguousarray(np.asarray(L0_attn_Wconv, np.float32)
                                  .reshape(NKD, 128, K4).transpose(1, 0, 2)
                                  .reshape(128, NKD * K4))
    bconv2 = np.ascontiguousarray(np.asarray(L0_attn_bconv, np.float32)
                                  .reshape(NKD, 128).T)
    wx2 = np.ascontiguousarray(np.asarray(L0_attn_Wx, np.float32).T
                               .reshape(NKD, 128, 160).transpose(1, 0, 2)
                               .reshape(128, NKD * 160))
    wdtT = np.ascontiguousarray(np.asarray(L0_attn_Wdt, np.float32).T)
    bdt2 = np.ascontiguousarray(np.asarray(L0_attn_bdt, np.float32)
                                .reshape(NKD, 128).T)
    woutT = np.ascontiguousarray(np.asarray(L0_attn_Wout, np.float32).T)
    alog2 = np.ascontiguousarray(np.asarray(L0_attn_A_log, np.float32)
                                 .reshape(NKD, 128, N16).transpose(1, 0, 2)
                                 .reshape(128, NKD * N16))
    dvec2 = np.ascontiguousarray(np.asarray(L0_attn_D, np.float32)
                                 .reshape(NKD, 128).T)
    cfc = np.ascontiguousarray(np.asarray(L0_ffn_c_fc, np.float32))
    cprojT = np.ascontiguousarray(np.asarray(L0_ffn_c_proj, np.float32).T)
    eye = np.eye(128, dtype=np.float32)

    nc = build_nc()
    in_maps = []
    for c in range(8):
        in_maps.append(dict(x=np.ascontiguousarray(x[c]), winT=winT,
                            wconv2=wconv2, bconv2=bconv2, wx2=wx2, wdtT=wdtT,
                            bdt2=bdt2, woutT=woutT, alog2=alog2, dvec2=dvec2,
                            cfc=cfc, cprojT=cprojT, eye=eye))
    res = run_bass_kernel_spmd(nc, in_maps, core_ids=list(range(8)))
    if DEBUG:
        kernel._last_res = res
    return np.stack([r["out"] for r in res.results], axis=0)


# version 04
# speedup vs baseline: 1.0102x; 1.0102x over previous
import contextlib
import numpy as np
import concourse.bass as bass
import concourse.mybir as mybir
from concourse.bass_utils import run_bass_kernel_spmd

F32 = mybir.dt.float32
F16 = mybir.dt.float16
BF16 = mybir.dt.bfloat16
AF = mybir.ActivationFunctionType
ALU = mybir.AluOpType
AXX = mybir.AxisListType.X

H = 2048
DI = 4096
E2 = 2 * DI
N16 = 16
K4 = 4
R = 128
L = 2048
HF = 8192
NQ = 8
LQ = 256
T2 = 2
NKH = 16
NKD = 32
NKF = 64
EPS = 1e-5
DEBUG = False


class Sem:
    def __init__(self, handle):
        self.h = handle
        self.count = 0


class Prog:
    ENGS = ("sync", "vector", "scalar", "tensor", "gpsimd")

    def __init__(self):
        self.ops = {e: [] for e in self.ENGS}

    def emit(self, eng, builder, waits=(), inc=None):
        for s, v in waits:
            assert v <= s.count, f"wait {v} > scheduled {s.count} on {eng}"
        if inc is not None:
            inc[0].count += inc[1]
            incv = (inc[0].h, inc[1])
        else:
            incv = None
        self.ops[eng].append((list(waits), builder, incv))

    def replay(self, eng_name, eng):
        for waits, builder, incv in self.ops[eng_name]:
            for s, v in waits:
                if v > 0:
                    eng.wait_ge(s.h, v)
            ins = builder(eng)
            if incv is not None:
                ins.then_inc(incv[0], incv[1])


def build_nc():
    nc = bass.Bass("TRN2", target_bir_lowering=False, debug=False)

    dx = nc.dram_tensor("x", [L, H], F32, kind="ExternalInput").ap()
    dwinT = nc.dram_tensor("winT", [H, E2], F32, kind="ExternalInput").ap()
    dwconv = nc.dram_tensor("wconv2", [128, NKD * K4], F32, kind="ExternalInput").ap()
    dbconv = nc.dram_tensor("bconv2", [128, NKD], F32, kind="ExternalInput").ap()
    dwx = nc.dram_tensor("wx2", [128, NKD * 160], F32, kind="ExternalInput").ap()
    dwdtT = nc.dram_tensor("wdtT", [R, DI], F32, kind="ExternalInput").ap()
    dbdt = nc.dram_tensor("bdt2", [128, NKD], F32, kind="ExternalInput").ap()
    dwoutT = nc.dram_tensor("woutT", [DI, H], F32, kind="ExternalInput").ap()
    dalog = nc.dram_tensor("alog2", [128, NKD * N16], F32, kind="ExternalInput").ap()
    ddvec = nc.dram_tensor("dvec2", [128, NKD], F32, kind="ExternalInput").ap()
    dcfc = nc.dram_tensor("cfc", [H, HF], F32, kind="ExternalInput").ap()
    dcprojT = nc.dram_tensor("cprojT", [HF, H], F32, kind="ExternalInput").ap()
    deye = nc.dram_tensor("eye", [128, 128], F32, kind="ExternalInput").ap()
    dout = nc.dram_tensor("out", [L, H], F32, kind="ExternalOutput").ap()

    s_win_h = nc.dram_tensor("s_win_h", [H, E2], BF16).ap()
    s_win_l = nc.dram_tensor("s_win_l", [H, E2], BF16).ap()
    s_wout_h = nc.dram_tensor("s_wout_h", [DI, H], BF16).ap()
    s_wout_l = nc.dram_tensor("s_wout_l", [DI, H], BF16).ap()
    s_q1 = nc.dram_tensor("s_q1", [H, HF], BF16,
                          kind="ExternalOutput" if DEBUG else "Internal").ap()
    s_q2 = nc.dram_tensor("s_q2", [HF, H], BF16,
                          kind="ExternalOutput" if DEBUG else "Internal").ap()
    s_sg = nc.dram_tensor("s_sg", [DI, L], F32).ap()
    s_s = nc.dram_tensor("s_s", [LQ, HF], F32).ap()
    s_bc = nc.dram_tensor("s_bc", [32, LQ], F32).ap()
    s_sc = nc.dram_tensor("s_sc", [1, 8], F32).ap()

    taps = {}
    if DEBUG:
        taps["t_hs"] = nc.dram_tensor("t_hs", [128, 8192], F32, kind="ExternalOutput").ap()
        taps["t_dtin"] = nc.dram_tensor("t_dtin", [128, 2 * LQ], BF16, kind="ExternalOutput").ap()
        taps["t_y2"] = nc.dram_tensor("t_y2", [128, 16384], BF16, kind="ExternalOutput").ap()
        taps["t_mo"] = nc.dram_tensor("t_mo", [128, 4096], F32, kind="ExternalOutput").ap()
        taps["t_x2"] = nc.dram_tensor("t_x2", [128, 4096], F32, kind="ExternalOutput").ap()
        taps["t_rfm"] = nc.dram_tensor("t_rfm", [128, 16384], BF16, kind="ExternalOutput").ap()
        taps["t_hst"] = nc.dram_tensor("t_hst", [128, 512], F32, kind="ExternalOutput").ap()
        taps["t_sc"] = nc.dram_tensor("t_sc", [128, 8], F32, kind="ExternalOutput").ap()
        taps["t_st"] = nc.dram_tensor("t_st", [128, 32], F32, kind="ExternalOutput").ap()
        taps["t_o6"] = nc.dram_tensor("t_o6", [128, 4096], F32, kind="ExternalOutput").ap()

    A = nc.alloc_sbuf_tensor
    eye_s = A("eye_s", [128, 128], F32)
    wconv_c = A("wconv_c", [128, NKD * K4], F32)
    bconv_c = A("bconv_c", [128, NKD], F32)
    bdt_c = A("bdt_c", [128, NKD], F32)
    dvec_c = A("dvec_c", [128, NKD], F32)
    nA_c = A("nA_c", [128, NKD * N16], F32)
    sc_col = A("sc_col", [128, 8], F32)
    st = A("st", [128, 32], F32)
    accum = A("accum", [128, 96], F32)
    wx_hi = A("wx_hi", [128, NKD * 160], BF16)
    wx_lo = A("wx_lo", [128, NKD * 160], BF16)
    wdt_hi = A("wdt_hi", [128, DI], BF16)
    wdt_lo = A("wdt_lo", [128, DI], BF16)
    hstate = A("hstate", [128, NKD * N16], F32)
    hprev = A("hprev", [128, NKD * 3], F32)
    scr2k = A("scr2k", [128, H], F32)
    out_t = A("out_t", [128, H], F32)
    xq = A("xq", [128, T2 * H], F32)
    x1f = A("x1f", [128, 2 * NKH * LQ], BF16)
    WS = A("WS", [128, 16384], BF16)
    hs = A("hs", [128, NKD * LQ], F32)
    hsx = A("hsx", [128, LQ + 4], F32)
    hs_h = A("hs_h", [128, LQ], BF16)
    hs_l = A("hs_l", [128, LQ], BF16)
    cvt = A("cvt", [128, LQ], F32)
    sgt = A("sgt", [128, 2 * 128], F32)
    sg_c = A("sg_c", [128, 2 * LQ], F32)
    dtin_h = A("dtin_h", [128, LQ], BF16)
    dtin_l = A("dtin_l", [128, LQ], BF16)
    dt_t = A("dt_t", [128, LQ], F32)
    u_t = A("u_t", [128, LQ], F32)
    da_t = A("da_t", [128, 2 * LQ], F32)
    dbu_t = A("dbu_t", [128, 2 * LQ], F32)
    h_t = A("h_t", [128, 2 * LQ], F32)
    y_t = A("y_t", [128, LQ], F32)
    yt2 = A("yt2", [128, LQ], F32)
    pj_t = A("pj_t", [128, 512], F32)
    bc_t = A("bc_t", [128, LQ], F32)
    rg_t = A("rg_t", [128, 512], F32)
    sq_t = A("sq_t", [128, 2 * 512], F32)
    sload = A("sload", [128, 2 * 512], F32)
    v16 = A("v16", [128, 512], F16)
    x2 = A("x2", [128, T2 * H], F32)
    csts = A("csts", [128, 2], F32)

    def MO(a, b):
        return x1f[:, 2 * a:2 * b].bitcast(F32)

    def BB(a, b):
        return WS[:, 2 * a:2 * b].bitcast(F32)

    def BBS(n):
        if n < 8:
            return out_t[:, n * LQ:(n + 1) * LQ]
        if n < 12:
            return sq_t[:, (n - 8) * LQ:(n - 7) * LQ]
        return sload[:, (n - 12) * LQ:(n - 11) * LQ]

    def CBS(n):
        return x2[:, n * LQ:(n + 1) * LQ]

    def HSB(a, b):
        return hs[:, a // 2:b // 2].bitcast(BF16)

    def WSF16(a, b):
        return WS[:, a:b].bitcast(F16)

    P = nc.alloc_psum_tensor
    ps = [P(f"ps{i}", [128, 512], F32) for i in range(8)]

    prog = Prog()
    with contextlib.ExitStack() as stack:
        def mksem(name):
            return Sem(stack.enter_context(nc.semaphore(name)))

        def mkpool(name, n):
            return [mksem(f"{name}{i}") for i in range(n)]

        qs, qg, qc = mksem("q_sync"), mksem("q_gp"), mksem("q_sc")
        ev, es, et = mksem("e_v"), mksem("e_s"), mksem("e_t")
        PWH = mkpool("pwh", 16)
        PM6 = mkpool("pm6", 16)
        PWL = mkpool("pwl", 16)
        PP = mkpool("pp", 2)
        PPW = mkpool("ppw", 2)
        PXL = mkpool("pxl", 2)
        PBB = mkpool("pbb", 1)
        PBN = mkpool("pbn", 1)
        PSM = mkpool("psm", 1)
        PSGR = mkpool("psgr", 2)
        PSLR = mkpool("pslr", 2)
        PSPG = mkpool("pspg", 2)
        PSPQ = mkpool("pspq", 2)
        PST = mkpool("pst", 2)
        PTAP = mkpool("ptap", 1)
        block = stack.enter_context(nc.Block())

        DMAQ = {"sync": qs, "gpsimd": qg, "scalar": qc}
        EPROG = {"vector": ev, "scalar": es, "tensor": et}

        def dma(queue, out, in_, waits=(), sem=None):
            q = sem if sem is not None else DMAQ[queue]
            prog.emit(queue, lambda e, o=out, i=in_: e.dma_start(out=o, in_=i),
                      waits=waits, inc=(q, 16))
            return (q, q.count)

        def op(eng, builder, waits=()):
            p = EPROG[eng]
            prog.emit(eng, builder, waits=waits, inc=(p, 1))
            return (p, p.count)

        dma("sync", eye_s[:], deye[:, :], sem=PSM[0])
        dma("sync", wconv_c[:], dwconv[:, :], sem=PSM[0])
        dma("sync", bconv_c[:], dbconv[:, :], sem=PSM[0])
        dma("sync", bdt_c[:], dbdt[:, :], sem=PSM[0])
        dma("sync", dvec_c[:], ddvec[:, :], sem=PSM[0])
        m_small = dma("sync", nA_c[:], dalog[:, :], sem=PSM[0])
        m_eye = m_wc = m_bc = m_bd = m_dv = m_al = m_small
        m1 = op("scalar", lambda e: e.activation(out=nA_c[:], in_=nA_c[:],
                                                 func=AF.Exp), waits=[m_al])
        m_nA = op("vector", lambda e: e.tensor_scalar_mul(nA_c[:], nA_c[:], -1.0),
                  waits=[m1])
        m_hs0 = op("vector", lambda e: e.memset(hstate[:], 0.0))
        m_hp0 = op("vector", lambda e: e.memset(hprev[:], 0.0))
        op("vector", lambda e: e.memset(csts[:, 0:1], EPS))
        m_cst = op("vector", lambda e: e.memset(csts[:, 1:2], 1.0))

        m_wx = dma("sync", hs[:, 0:NKD * 160], dwx[:, :], sem=PP[0])
        m_wxh = op("scalar", lambda e: e.copy(wx_hi[:], hs[:, 0:NKD * 160]),
                   waits=[m_wx])
        m_wxl = op("vector", lambda e: e.tensor_tensor(
            wx_lo[:], hs[:, 0:NKD * 160], wx_hi[:], ALU.subtract), waits=[m_wxh])
        m_wdt = dma("sync", hs[:, 0:DI], dwdtT[:, :], waits=[m_wxh, m_wxl],
                    sem=PP[1])
        m_wdh = op("scalar", lambda e: e.copy(wdt_hi[:], hs[:, 0:DI]),
                   waits=[m_wdt])
        m_wdl = op("vector", lambda e: e.tensor_tensor(
            wdt_lo[:], hs[:, 0:DI], wdt_hi[:], ALU.subtract),
            waits=[m_wdh])

        last_v, last_s, last_wr = m_wxl, m_wdh, None

        def split_weight(dsrc, nk, ncols, dsth, dstl):
            nonlocal last_v, last_s, last_wr
            for k in range(nk):
                w = [last_v, last_s, m_wdl]
                if last_wr is not None:
                    w.append(last_wr)
                mld = dma("sync", hs[:, 0:ncols], dsrc[k * 128:(k + 1) * 128, :],
                          waits=w, sem=PP[k % 2])
                mh = op("scalar", lambda e, n_=ncols: e.copy(
                    WS[:, 0:n_], hs[:, 0:n_]), waits=[mld])
                ml = op("vector", lambda e, n_=ncols: e.tensor_tensor(
                    WS[:, 8192:8192 + n_], hs[:, 0:n_], WS[:, 0:n_],
                    ALU.subtract), waits=[mh])
                dma("gpsimd", dsth[k * 128:(k + 1) * 128, :], WS[:, 0:ncols],
                    waits=[mh], sem=PPW[0])
                mw2 = dma("gpsimd", dstl[k * 128:(k + 1) * 128, :],
                          WS[:, 8192:8192 + ncols], waits=[ml], sem=PPW[1])
                last_v, last_s, last_wr = ml, mh, mw2

        split_weight(dwinT, NKH, E2, s_win_h, s_win_l)
        split_weight(dwoutT, NKD, H, s_wout_h, s_wout_l)

        def ternary_prep(dsrc, nk, ncols, dstq, acc0, sc_i):
            nonlocal last_v, last_s, last_wr
            pass1 = []
            for k in range(nk):
                w = [last_v, last_s]
                if last_wr is not None:
                    w.append(last_wr)
                mld = dma("sync", hs[:, 0:ncols], dsrc[k * 128:(k + 1) * 128, :],
                          waits=w, sem=PP[k % 2])
                ma = op("scalar", lambda e, n_=ncols, k=k: e.activation(
                    out=BB(0, n_), in_=hs[:, 0:n_], func=AF.Abs,
                    accum_out=accum[:, acc0 + k:acc0 + k + 1]), waits=[mld])
                last_v, last_s, last_wr = ma, ma, None
                pass1.append(ma)
            mr = op("vector", lambda e: e.reduce_sum(
                accum[:, 88:89], accum[:, acc0:acc0 + nk], axis=AXX),
                waits=[pass1[-1]])
            mo_ = op("vector", lambda e: e.memset(accum[:, 89:90], 1.0))
            mps = op("tensor", lambda e: e.matmul(
                ps[6][0:1, 0:1], accum[:, 89:90], accum[:, 88:89],
                start=True, stop=True), waits=[mr, mo_])
            mv = op("vector", lambda e, nn=nk * 128 * ncols: e.tensor_scalar(
                accum[0:1, 90:91], ps[6][0:1, 0:1], 1.0 / nn, EPS,
                ALU.mult, ALU.max), waits=[mps])
            mb = dma("gpsimd", s_sc[0:1, sc_i:sc_i + 1], accum[0:1, 90:91],
                     waits=[mv], sem=PBN[0])
            mbc = dma("gpsimd", sc_col[:, sc_i:sc_i + 1],
                      s_sc[0:1, sc_i:sc_i + 1].partition_broadcast(128),
                      waits=[mb], sem=PBN[0])
            minv = op("vector", lambda e, i=sc_i: e.reciprocal(
                sc_col[:, i + 4:i + 5], sc_col[:, i:i + 1]), waits=[mbc])
            for k in range(nk):
                w = [last_v, last_s, minv]
                if last_wr is not None:
                    w.append(last_wr)
                mld = dma("sync", hs[:, 0:ncols], dsrc[k * 128:(k + 1) * 128, :],
                          waits=w, sem=PP[k % 2])
                mq0 = op("vector", lambda e, n_=ncols, i=sc_i: e.tensor_scalar(
                    hs[:, 0:n_], hs[:, 0:n_], sc_col[:, i + 4:i + 5], None,
                    ALU.mult), waits=[mld])
                mq1 = op("vector", lambda e, n_=ncols: e.tensor_scalar(
                    WSF16(0, n_), hs[:, 0:n_], 0.5, None, ALU.is_gt),
                    waits=[mq0])
                mq2 = op("vector", lambda e, n_=ncols: e.tensor_scalar(
                    hs[:, 0:n_], hs[:, 0:n_], -0.5, None, ALU.is_lt),
                    waits=[mq1])
                mq3 = op("vector", lambda e, n_=ncols: e.tensor_tensor(
                    WS[:, 8192:8192 + n_], WSF16(0, n_), hs[:, 0:n_],
                    ALU.subtract), waits=[mq2])
                mwq = dma("gpsimd", dstq[k * 128:(k + 1) * 128, :],
                          WS[:, 8192:8192 + ncols], waits=[mq3], sem=PPW[k % 2])
                last_v, last_s, last_wr = mq3, mq3, mwq
            return mbc

        mbc1 = ternary_prep(dcfc, NKH, HF, s_q1, 0, 0)
        mbc2 = ternary_prep(dcprojT, NKF, H, s_q2, 16, 1)
        prep_last = [(PPW[0], PPW[0].count), (PPW[1], PPW[1].count)]
        md1 = op("vector", lambda e: e.tensor_tensor(
            sc_col[:, 2:3], sc_col[:, 0:1], sc_col[:, 0:1], ALU.mult),
            waits=[mbc1])
        md2 = op("vector", lambda e: e.tensor_scalar(
            sc_col[:, 3:4], sc_col[:, 2:3], 255.0, None, ALU.mult), waits=[md1])
        md3 = op("vector", lambda e: e.tensor_scalar(
            sc_col[:, 6:7], sc_col[:, 1:2], 1.0 / 255.0, None, ALU.mult),
            waits=[mbc2])
        tapm = {}
        if DEBUG:
            tapm["sc"] = dma("gpsimd", taps["t_sc"][:, :], sc_col[:],
                             waits=[md1, md2, md3], sem=PTAP[0])

        def ln_stats(xsrc, si, extra_waits=()):
            w = list(extra_waits)
            ms1 = op("vector", lambda e: e.reduce_sum(st[:, si:si + 1], xsrc,
                                                      axis=AXX), waits=w)
            ms2 = op("scalar", lambda e: e.activation(
                out=out_t[:, 0:H], in_=xsrc, func=AF.Square,
                accum_out=st[:, si + 1:si + 2]), waits=w)
            ms3 = op("vector", lambda e: e.tensor_scalar_mul(
                st[:, si:si + 1], st[:, si:si + 1], 1.0 / H), waits=[ms1])
            ms4 = op("vector", lambda e: e.tensor_scalar_mul(
                st[:, si + 1:si + 2], st[:, si + 1:si + 2], 1.0 / H),
                waits=[ms2])
            ms5 = op("vector", lambda e: e.tensor_tensor(
                st[:, si + 2:si + 3], st[:, si:si + 1], st[:, si:si + 1],
                ALU.mult), waits=[ms3])
            ms6 = op("vector", lambda e: e.tensor_tensor(
                st[:, si + 1:si + 2], st[:, si + 1:si + 2],
                st[:, si + 2:si + 3], ALU.subtract), waits=[ms4, ms5])
            ms7 = op("scalar", lambda e: e.activation(
                out=st[:, si + 1:si + 2], in_=st[:, si + 1:si + 2],
                func=AF.Sqrt, bias=csts[:, 0:1]), waits=[ms6, m_cst])
            ms8 = op("vector", lambda e: e.reciprocal(
                st[:, si + 1:si + 2], st[:, si + 1:si + 2]), waits=[ms7])
            return ms8

        psum_free = {}

        def fm_split(t, waits_in):
            ml = None
            for hc in range(NKH):
                pb = ps[6] if hc % 2 == 0 else ps[7]
                tw = list(waits_in) + [m_eye]
                if psum_free.get(id(pb)) is not None:
                    tw.append(psum_free[id(pb)])
                mtr = op("tensor", lambda e, pb=pb, hc=hc: e.transpose(
                    pb[:, 0:128], scr2k[:, hc * 128:(hc + 1) * 128], eye_s[:]),
                    waits=tw)
                dh = x1f[:, hc * LQ + t * 128: hc * LQ + (t + 1) * 128]
                dl = x1f[:, 4096 + hc * LQ + t * 128: 4096 + hc * LQ + (t + 1) * 128]
                mh = op("scalar", lambda e, pb=pb, d=dh: e.activation(
                    out=d, in_=pb[:, 0:128], func=AF.Copy), waits=[mtr])
                ml = op("vector", lambda e, pb=pb, d=dl, s_=dh: e.tensor_tensor(
                    d, pb[:, 0:128], s_, ALU.subtract), waits=[mh])
                psum_free[id(pb)] = ml
            return ml

        xq_free = None
        hs_free = None
        ws_free = None
        x1_free = None
        sgt_free = {}
        sgc_free = {}
        sq_free = {}
        sload_free = {}
        out_dma_marks = [None, None]
        prev_store = None
        first_q = True

        for q in range(NQ):
            lo = q * LQ
            mx = []
            for t in range(T2):
                w = []
                if xq_free is not None:
                    w.append(xq_free)
                if prev_store is not None:
                    w.append(prev_store)
                mx.append(dma("gpsimd", xq[:, t * H:(t + 1) * H],
                              dx[lo + t * 128:lo + (t + 1) * 128, :], waits=w,
                              sem=PXL[t]))
            m_x1 = []
            prev_split = None
            for t in range(T2):
                xs = xq[:, t * H:(t + 1) * H]
                lw = [mx[t]]
                if out_dma_marks[t] is not None:
                    lw.append(out_dma_marks[t])
                ms = ln_stats(xs, 0, extra_waits=lw)
                maw = [ms]
                if prev_split is not None:
                    maw.append(prev_split)
                if x1_free is not None and t == 0:
                    maw.append(x1_free)
                ma = op("vector", lambda e, xs=xs: e.tensor_scalar(
                    scr2k[:], xs, st[:, 0:1], st[:, 1:2], ALU.subtract,
                    ALU.mult), waits=maw)
                prev_split = fm_split(t, [ma])
                m_x1.append(prev_split)

            ws_slot = {}
            conv_marks = {}
            gate_spill = {}
            hs_piece = {}
            m2_last = None
            hs_gate_done = None
            last_m1_mm = None

            def m1_eblock(eb):
                nonlocal last_m1_mm
                mm_last = {}
                for k in range(NKH):
                    so = k * 1024
                    w = []
                    if first_q and eb == 0 and k == 0:
                        w += prep_last
                    if eb == 0 and k == 0 and ws_free is not None:
                        w.append(ws_free)
                    if ws_slot.get(k) is not None:
                        w.append(ws_slot[k])
                    mh_l = dma("sync", WS[:, so:so + 512],
                               s_win_h[k * 128:(k + 1) * 128,
                                       eb * 512:(eb + 1) * 512], waits=w,
                               sem=PWH[k])
                    ml_l = dma("scalar", WS[:, so + 512:so + 1024],
                               s_win_l[k * 128:(k + 1) * 128,
                                       eb * 512:(eb + 1) * 512], waits=w,
                               sem=PWL[k])
                    for t in range(T2):
                        pb = ps[t]
                        pw = [mh_l, ml_l, m_x1[t]]
                        if k == 0 and psum_free.get(id(pb)) is not None:
                            pw.append(psum_free[id(pb)])
                        xh = x1f[:, k * LQ + t * 128: k * LQ + (t + 1) * 128]
                        xl = x1f[:, 4096 + k * LQ + t * 128:
                                 4096 + k * LQ + (t + 1) * 128]
                        wh = WS[:, so:so + 512]
                        wl = WS[:, so + 512:so + 1024]
                        op("tensor", lambda e, pb=pb, xh=xh, wh=wh, s_=(k == 0):
                           e.matmul(pb[:, 0:512], xh, wh, start=s_, stop=False),
                           waits=pw if k == 0 else [mh_l, ml_l])
                        op("tensor", lambda e, pb=pb, xh=xh, wl=wl:
                           e.matmul(pb[:, 0:512], xh, wl, start=False,
                                    stop=False))
                        mm = op("tensor", lambda e, pb=pb, xl=xl, wh=wh,
                                p_=(k == NKH - 1):
                                e.matmul(pb[:, 0:512], xl, wh, start=False,
                                         stop=p_))
                        mm_last[t] = mm
                    ws_slot[k] = mm_last[T2 - 1]
                    last_m1_mm = mm_last[T2 - 1]
                return mm_last

            def m1_evac(eb, t, mm_t, prev_tr):
                ew_ = [mm_t]
                if prev_tr is not None:
                    ew_.append(prev_tr)
                me_ = op("scalar", lambda e, pb=ps[t]: e.activation(
                    out=pj_t[:], in_=pb[:, 0:512], func=AF.Copy), waits=ew_)
                psum_free[id(ps[t])] = me_
                last_tr = None
                for j in range(4):
                    c = eb * 4 + j
                    pb2 = ps[6] if j % 2 == 0 else ps[7]
                    tw = [me_]
                    if psum_free.get(id(pb2)) is not None:
                        tw.append(psum_free[id(pb2)])
                    mtr = op("tensor", lambda e, pb2=pb2, j=j: e.transpose(
                        pb2[:, 0:128], pj_t[:, j * 128:(j + 1) * 128], eye_s[:]),
                        waits=tw)
                    last_tr = mtr
                    if c < NKD:
                        hw = [mtr]
                        if hs_free is not None and eb == 0 and j == 0 and t == 0:
                            hw.append(hs_free)
                            if tapm.get("rfm") is not None:
                                hw.append(tapm["rfm"])
                        mev = op("scalar", lambda e, pb2=pb2, c=c, t=t:
                                 e.copy(hs[:, c * LQ + t * 128:
                                           c * LQ + (t + 1) * 128],
                                        pb2[:, 0:128]), waits=hw)
                        hs_piece.setdefault(c, []).append(mev)
                    else:
                        g = c - NKD
                        sb = (g * 2 + t) % 2
                        gw = [mtr]
                        if sgt_free.get(sb) is not None:
                            gw.append(sgt_free[sb])
                        mev = op("scalar", lambda e, pb2=pb2, sb=sb: e.activation(
                            out=sgt[:, sb * 128:(sb + 1) * 128],
                            in_=pb2[:, 0:128], func=AF.Silu), waits=gw)
                        sgt_free[sb] = dma(
                            "gpsimd",
                            s_sg[g * 128:(g + 1) * 128,
                                 lo + t * 128:lo + (t + 1) * 128],
                            sgt[:, sb * 128:(sb + 1) * 128], waits=[mev],
                            sem=PSPG[sb])
                        gate_spill[(g, t)] = sgt_free[sb]
                    psum_free[id(pb2)] = mev
                return last_tr

            prev_tr = None
            for eb in range(8):
                mm_last = m1_eblock(eb)
                for t in range(T2):
                    prev_tr = m1_evac(eb, t, mm_last[t], prev_tr)
                for j in range(4):
                    c = eb * 4 + j
                    pw = list(hs_piece[c])
                    mh1 = op("scalar", lambda e, c=c: e.copy(
                        hsx[:, 0:3], hprev[:, c * 3:(c + 1) * 3]),
                        waits=pw + [m_hp0])
                    mh2 = op("scalar", lambda e, c=c: e.copy(
                        hsx[:, 3:LQ + 3], hs[:, c * LQ:(c + 1) * LQ]), waits=pw)
                    mh3 = op("scalar", lambda e, c=c: e.copy(
                        hprev[:, c * 3:(c + 1) * 3], hsx[:, LQ:LQ + 3]),
                        waits=[mh2])
                    mc = op("vector", lambda e, c=c: e.tensor_scalar(
                        cvt[:], hsx[:, 0:LQ], wconv_c[:, c * K4:c * K4 + 1],
                        None, ALU.mult), waits=[mh1, mh2, m_wc])
                    for kk in range(1, K4):
                        mc = op("vector", lambda e, c=c, kk=kk:
                                e.scalar_tensor_tensor(
                                    cvt[:], hsx[:, kk:kk + LQ],
                                    wconv_c[:, c * K4 + kk:c * K4 + kk + 1],
                                    cvt[:], ALU.mult, ALU.add), waits=[mc])
                    mhs = op("scalar", lambda e, c=c: e.activation(
                        out=hs[:, c * LQ:(c + 1) * LQ], in_=cvt[:],
                        func=AF.Silu, bias=bconv_c[:, c:c + 1]),
                        waits=[mc, m_bc, mh3])
                    conv_marks[c] = mhs
                    hhw = [mhs]
                    if m2_last is not None:
                        hhw.append(m2_last)
                    mhh = op("scalar", lambda e, c=c: e.copy(
                        hs_h[:], hs[:, c * LQ:(c + 1) * LQ]), waits=hhw)
                    mhl = op("vector", lambda e, c=c: e.tensor_tensor(
                        hs_l[:], hs[:, c * LQ:(c + 1) * LQ], hs_h[:],
                        ALU.subtract), waits=[mhh])
                    m2w = [mhh, mhl, m_wxl]
                    if c == 0:
                        for b in (2, 3):
                            if psum_free.get(id(ps[b])) is not None:
                                m2w.append(psum_free[id(ps[b])])
                    for pi, (wt, xt) in enumerate(
                            [(wx_hi, hs_h), (wx_hi, hs_l), (wx_lo, hs_h)]):
                        s_ = (c == 0 and pi == 0)
                        p_ = (c == NKD - 1 and pi == 2)
                        op("tensor", lambda e, w=wt[:, c * 160:c * 160 + 128],
                           xt=xt, s_=s_, p_=p_:
                           e.matmul(ps[2][:, 0:LQ], w, xt[:], start=s_,
                                    stop=p_), waits=m2w if pi == 0 else [])
                        mmb = op("tensor",
                                 lambda e, w=wt[:, c * 160 + 128:(c + 1) * 160],
                                 xt=xt, s_=s_, p_=p_:
                                 e.matmul(ps[3][0:32, 0:LQ], w, xt[:],
                                          start=s_, stop=p_))
                    m2_last = mmb

            for eb in range(8, 16):
                mm_last = m1_eblock(eb)
                for t in range(T2):
                    prev_tr = m1_evac(eb, t, mm_last[t], prev_tr)
            spg_fin = (sgt_free[0], sgt_free[1])

            mdh = op("scalar", lambda e: e.activation(
                out=dtin_h[:], in_=ps[2][:, 0:LQ], func=AF.Copy),
                waits=[m2_last])
            mdl = op("vector", lambda e: e.tensor_tensor(
                dtin_l[:], ps[2][:, 0:LQ], dtin_h[:], ALU.subtract),
                waits=[mdh])
            psum_free[id(ps[2])] = mdl
            mbce = op("vector", lambda e: e.tensor_copy(
                bc_t[0:32, 0:LQ], ps[3][0:32, 0:LQ]), waits=[m2_last])
            psum_free[id(ps[3])] = mbce
            mbcd = dma("gpsimd", s_bc[:, :], bc_t[0:32, 0:LQ], waits=[mbce],
                       sem=PBN[0])
            bb_w = [mbcd]
            for t_ in range(T2):
                if out_dma_marks[t_] is not None:
                    bb_w.append(out_dma_marks[t_])
            for sb_ in (0, 1):
                if sq_free.get(sb_) is not None:
                    bb_w.append(sq_free[sb_])
                if sload_free.get(sb_) is not None:
                    bb_w.append(sload_free[sb_])
            if prev_store is not None:
                bb_w.append(prev_store)
            m_bb = None
            for n in range(N16):
                dma("gpsimd", BBS(n),
                    s_bc[n:n + 1, :].partition_broadcast(128), waits=bb_w,
                    sem=PBB[0])
                m_bb = dma("gpsimd", CBS(n),
                           s_bc[N16 + n:N16 + n + 1, :].partition_broadcast(128),
                           waits=[mbcd], sem=PBB[0])

            if DEBUG and q == 0:
                tapm["hs"] = dma("gpsimd", taps["t_hs"][:, :], hs[:, 0:8192],
                                 waits=[conv_marks[c] for c in range(NKD)],
                                 sem=PTAP[0])
                dma("gpsimd", taps["t_dtin"][:, 0:LQ], dtin_h[:], waits=[mdl],
                    sem=PTAP[0])
                tapm["dtin"] = dma("gpsimd", taps["t_dtin"][:, LQ:2 * LQ],
                                   dtin_l[:], waits=[mdl], sem=PTAP[0])

            y2_marks = []
            scan_last = None
            prev_mu = None
            prev_m2h = None
            prev_msv = {}
            for c in range(NKD):
                pb = ps[4 + c % 2]
                m3w = [mdh, mdl, m_wdl]
                if psum_free.get(id(pb)) is not None:
                    m3w.append(psum_free[id(pb)])
                mm3 = None
                for pi, (wt_, xt_) in enumerate([(wdt_hi, dtin_h),
                                                 (wdt_hi, dtin_l),
                                                 (wdt_lo, dtin_h)]):
                    mm3 = op("tensor", lambda e, pb=pb,
                             w=wt_[:, c * 128:(c + 1) * 128], x=xt_,
                             s_=(pi == 0), p_=(pi == 2):
                             e.matmul(pb[:, 0:LQ], w, x[:], start=s_, stop=p_),
                             waits=m3w if pi == 0 else [])
                s1w = [mm3, m_bd]
                if prev_mu is not None:
                    s1w.append(prev_mu)
                ms1 = op("scalar", lambda e, pb=pb, c=c: e.activation(
                    out=dt_t[:], in_=pb[:, 0:LQ], func=AF.Exp,
                    bias=bdt_c[:, c:c + 1]), waits=s1w)
                psum_free[id(pb)] = ms1
                ms2 = op("scalar", lambda e: e.activation(
                    out=dt_t[:], in_=dt_t[:], func=AF.Ln, bias=csts[:, 1:2]),
                    waits=[ms1, m_cst])
                mu = op("vector", lambda e, c=c: e.tensor_tensor(
                    u_t[:], dt_t[:], hs[:, c * LQ:(c + 1) * LQ], ALU.mult),
                    waits=[ms2, conv_marks[c]])
                prev_mu = mu
                sb = c % 2
                gw = [spg_fin[0], spg_fin[1]]
                if sgc_free.get(sb) is not None:
                    gw.append(sgc_free[sb])
                msg = dma("gpsimd", sg_c[:, sb * LQ:(sb + 1) * LQ],
                          s_sg[c * 128:(c + 1) * 128, lo:lo + LQ], waits=gw,
                          sem=PSGR[sb])
                my = None
                prev_y = {}
                for n in range(N16):
                    p2 = n % 2
                    daw = [ms2, m_nA]
                    if prev_y.get(p2) is not None:
                        daw.append(prev_y[p2])
                    yt2w = [prev_m2h] if (n == 1 and prev_m2h is not None) else []
                    mda = op("scalar", lambda e, c=c, n=n, p2=p2: e.activation(
                        out=da_t[:, p2 * LQ:(p2 + 1) * LQ], in_=dt_t[:],
                        func=AF.Exp,
                        scale=nA_c[:, c * N16 + n:c * N16 + n + 1]), waits=daw)
                    mdb = op("vector", lambda e, n=n, p2=p2: e.tensor_tensor(
                        dbu_t[:, p2 * LQ:(p2 + 1) * LQ], u_t[:],
                        BBS(n), ALU.mult),
                        waits=[mu, m_bb])
                    scw = [mda, mdb, m_hs0]
                    if prev_msv.get(p2) is not None:
                        scw.append(prev_msv[p2])
                    if prev_msv.get(2 + (n + 1) % 2) is not None:
                        scw.append(prev_msv[2 + (n + 1) % 2])
                    msc = op("vector", lambda e, c=c, n=n, p2=p2:
                             e.tensor_tensor_scan(
                                 h_t[:, p2 * LQ:(p2 + 1) * LQ],
                                 da_t[:, p2 * LQ:(p2 + 1) * LQ],
                                 dbu_t[:, p2 * LQ:(p2 + 1) * LQ],
                                 hstate[:, c * N16 + n:c * N16 + n + 1],
                                 ALU.mult, ALU.add), waits=scw)
                    prev_msv[p2] = op("scalar", lambda e, c=c, n=n, p2=p2: e.copy(
                        hstate[:, c * N16 + n:c * N16 + n + 1],
                        h_t[:, p2 * LQ + LQ - 1:p2 * LQ + LQ]), waits=[msc])
                    if n == 0:
                        my = op("vector", lambda e, n=n, p2=p2: e.tensor_tensor(
                            y_t[:], h_t[:, p2 * LQ:(p2 + 1) * LQ],
                            CBS(n), ALU.mult),
                            waits=[msc])
                    else:
                        mt_ = op("vector", lambda e, n=n, p2=p2: e.tensor_tensor(
                            yt2[:], h_t[:, p2 * LQ:(p2 + 1) * LQ],
                            CBS(n), ALU.mult),
                            waits=[msc] + yt2w)
                        my = op("vector", lambda e: e.tensor_tensor(
                            y_t[:], y_t[:], yt2[:], ALU.add), waits=[mt_])
                    prev_y[p2] = my
                my2 = op("vector", lambda e, c=c: e.scalar_tensor_tensor(
                    y_t[:], hs[:, c * LQ:(c + 1) * LQ], dvec_c[:, c:c + 1],
                    y_t[:], ALU.mult, ALU.add), waits=[my, m_dv])
                my3 = op("vector", lambda e, sb=sb: e.tensor_tensor(
                    yt2[:], y_t[:], sg_c[:, sb * LQ:(sb + 1) * LQ], ALU.mult),
                    waits=[my2, msg])
                sgc_free[sb] = my3
                h2w = [my3]
                if c == 0 and tapm.get("hs") is not None:
                    h2w.append(tapm["hs"])
                m2h = op("scalar", lambda e, c=c: e.copy(
                    HSB(c * 2 * LQ, c * 2 * LQ + LQ), yt2[:]), waits=h2w)
                prev_m2h = m2h
                m2l = op("vector", lambda e, c=c: e.tensor_tensor(
                    HSB(c * 2 * LQ + LQ, (c + 1) * 2 * LQ), yt2[:],
                    HSB(c * 2 * LQ, c * 2 * LQ + LQ), ALU.subtract),
                    waits=[m2h])
                y2_marks.append(m2l)
                scan_last = m2l

            if DEBUG and q == 0:
                tapm["y2"] = dma("gpsimd", taps["t_y2"][:, :], HSB(0, 16384),
                                 waits=[scan_last], sem=PTAP[0])
                tapm["hst"] = dma("gpsimd", taps["t_hst"][:, :], hstate[:],
                                  waits=[scan_last], sem=PTAP[0])

            ws_slot_c = {}
            mo_done = [[], []]
            m4_last_mm = None
            for hb in range(4):
                mm_last = {}
                for k in range(NKD):
                    slot = k % 8
                    so = slot * 2048
                    w = [y2_marks[k]]
                    if first_q and hb == 0 and k == 0:
                        w.extend(prep_last)
                    if hb == 0 and k < 8:
                        w.append(ws_slot[2 * (k % 8)])
                    if ws_slot_c.get(slot) is not None:
                        w.append(ws_slot_c[slot])
                    mh_l = dma("sync", WS[:, so:so + 512],
                               s_wout_h[k * 128:(k + 1) * 128,
                                        hb * 512:(hb + 1) * 512], waits=w,
                               sem=PWH[k % 16])
                    ml_l = dma("scalar", WS[:, so + 512:so + 1024],
                               s_wout_l[k * 128:(k + 1) * 128,
                                        hb * 512:(hb + 1) * 512], waits=w,
                               sem=PWL[k % 16])
                    for t in range(T2):
                        pb = ps[t]
                        pw = [mh_l, ml_l]
                        if k == 0 and psum_free.get(id(pb)) is not None:
                            pw.append(psum_free[id(pb)])
                        yh = HSB(k * 2 * LQ + t * 128, k * 2 * LQ + (t + 1) * 128)
                        yl = HSB(k * 2 * LQ + LQ + t * 128,
                                 k * 2 * LQ + LQ + (t + 1) * 128)
                        wh = WS[:, so:so + 512]
                        wl = WS[:, so + 512:so + 1024]
                        op("tensor", lambda e, pb=pb, yh=yh, wh=wh, s_=(k == 0):
                           e.matmul(pb[:, 0:512], yh, wh, start=s_, stop=False),
                           waits=pw if k == 0 else [mh_l, ml_l])
                        op("tensor", lambda e, pb=pb, yh=yh, wl=wl:
                           e.matmul(pb[:, 0:512], yh, wl, start=False,
                                    stop=False))
                        mm = op("tensor", lambda e, pb=pb, yl=yl, wh=wh,
                                p_=(k == NKD - 1):
                                e.matmul(pb[:, 0:512], yl, wh, start=False,
                                         stop=p_))
                        mm_last[t] = mm
                    ws_slot_c[slot] = mm_last[T2 - 1]
                m4_last_mm = mm_last[T2 - 1]
                for t in range(T2):
                    pb = ps[t]
                    ew = [mm_last[t]]
                    if hb == 0 and t == 0:
                        ew.append(m_x1[1])
                        ew.append(last_m1_mm)
                    mev = op("vector", lambda e, pb=pb, t=t, hb=hb: e.tensor_copy(
                        MO(t * H + hb * 512, t * H + (hb + 1) * 512),
                        pb[:, 0:512]), waits=ew)
                    psum_free[id(pb)] = mev
                    mo_done[t].append(mev)
            if DEBUG and q == 0:
                tapm["mo"] = dma("gpsimd", taps["t_mo"][:, :], MO(0, 4096),
                                 waits=mo_done[0] + mo_done[1], sem=PTAP[0])
            x2_done = []
            for t in range(T2):
                xs = MO(t * H, (t + 1) * H)
                ms = ln_stats(xs, 4, extra_waits=list(mo_done[t]))
                ma = op("vector", lambda e, xs=xs: e.tensor_scalar(
                    scr2k[:], xs, st[:, 4:5], st[:, 5:6], ALU.subtract,
                    ALU.mult), waits=[ms])
                mr_ = op("vector", lambda e, t=t: e.tensor_tensor(
                    x2[:, t * H:(t + 1) * H], scr2k[:],
                    xq[:, t * H:(t + 1) * H], ALU.add), waits=[ma, mx[t]])
                x2_done.append(mr_)
            xq_free = x2_done[-1]

            if DEBUG and q == 0:
                tapm["x2"] = dma("gpsimd", taps["t_x2"][:, :], x2[:],
                                 waits=x2_done, sem=PTAP[0])

            m_x3 = []
            prev_split = None
            for t in range(T2):
                xs = x2[:, t * H:(t + 1) * H]
                ms = ln_stats(xs, 8, extra_waits=[x2_done[t]])
                maw = [ms]
                if t == 0:
                    maw.append(x2_done[1])
                    if tapm.get("mo") is not None:
                        maw.append(tapm["mo"])
                if prev_split is not None:
                    maw.append(prev_split)
                ma = op("vector", lambda e, xs=xs: e.tensor_scalar(
                    scr2k[:], xs, st[:, 8:9], st[:, 9:10], ALU.subtract,
                    ALU.mult), waits=maw)
                prev_split = fm_split(t, [ma])
                m_x3.append(prev_split)

            gmax_init = [op("vector", lambda e, t=t: e.memset(
                st[:, 12 + t:13 + t], 0.0)) for t in range(T2)]
            gmax_last = [gmax_init[0], gmax_init[1]]
            ws_slot_d = {}
            sq_spills = {}
            m5_last_mm = None
            prev_mrx = None
            for fb in range(16):
                mm_last = {}
                for k in range(NKH):
                    so = k * 512
                    w = [m_x3[0], m_x3[1]]
                    if first_q and fb == 0 and k == 0:
                        pass
                    if fb == 0:
                        w.append(m4_last_mm)
                    if ws_slot_d.get(k) is not None:
                        w.append(ws_slot_d[k])
                    mq_l = dma("sync", WS[:, so:so + 512],
                               s_q1[k * 128:(k + 1) * 128,
                                    fb * 512:(fb + 1) * 512], waits=w,
                               sem=PWH[k])
                    for t in range(T2):
                        pb = ps[2 + t]
                        pw = [mq_l]
                        if k == 0 and psum_free.get(id(pb)) is not None:
                            pw.append(psum_free[id(pb)])
                        xh = x1f[:, k * LQ + t * 128: k * LQ + (t + 1) * 128]
                        xl = x1f[:, 4096 + k * LQ + t * 128:
                                 4096 + k * LQ + (t + 1) * 128]
                        wq = WS[:, so:so + 512]
                        op("tensor", lambda e, pb=pb, xh=xh, wq=wq, s_=(k == 0):
                           e.matmul(pb[:, 0:512], xh, wq, start=s_, stop=False),
                           waits=pw if k == 0 else [mq_l])
                        mm = op("tensor", lambda e, pb=pb, xl=xl, wq=wq,
                                p_=(k == NKH - 1):
                                e.matmul(pb[:, 0:512], xl, wq, start=False,
                                         stop=p_))
                        mm_last[t] = mm
                    ws_slot_d[k] = mm_last[T2 - 1]
                m5_last_mm = mm_last[T2 - 1]
                for t in range(T2):
                    pb = ps[2 + t]
                    rlw = [mm_last[t]]
                    if prev_mrx is not None:
                        rlw.append(prev_mrx)
                    mrl = op("scalar", lambda e, pb=pb: e.activation(
                        out=rg_t[:], in_=pb[:, 0:512], func=AF.Relu),
                        waits=rlw)
                    psum_free[id(pb)] = mrl
                    mrx = op("vector", lambda e: e.reduce_max(
                        st[:, 16:17], rg_t[:], axis=AXX), waits=[mrl])
                    prev_mrx = mrx
                    mgm = op("vector", lambda e, t=t: e.tensor_tensor(
                        st[:, 12 + t:13 + t], st[:, 12 + t:13 + t],
                        st[:, 16:17], ALU.max), waits=[mrx, gmax_last[t]])
                    gmax_last[t] = mgm
                    sb = (fb * 2 + t) % 2
                    sqw = [mrl, mrx]
                    if sq_free.get(sb) is not None:
                        sqw.append(sq_free[sb])
                    msq = op("scalar", lambda e, sb=sb: e.activation(
                        out=sq_t[:, sb * 512:(sb + 1) * 512], in_=rg_t[:],
                        func=AF.Square), waits=sqw)
                    sq_free[sb] = dma("gpsimd",
                                      s_s[t * 128:(t + 1) * 128,
                                          fb * 512:(fb + 1) * 512],
                                      sq_t[:, sb * 512:(sb + 1) * 512],
                                      waits=[msq], sem=PSPQ[sb])
                    sq_spills[(fb, t)] = sq_free[sb]

            spq_fin = (sq_free[0], sq_free[1])

            fso = []
            for t in range(T2):
                w1 = op("vector", lambda e, t=t: e.tensor_tensor(
                    st[:, 16 + t:17 + t], st[:, 12 + t:13 + t],
                    st[:, 12 + t:13 + t], ALU.mult), waits=[gmax_last[t]])
                w2 = op("vector", lambda e, t=t: e.tensor_scalar(
                    st[:, 16 + t:17 + t], st[:, 16 + t:17 + t],
                    sc_col[:, 2:3], EPS, ALU.mult, ALU.max),
                    waits=[w1, md2])
                w3 = op("vector", lambda e, t=t: e.reciprocal(
                    st[:, 18 + t:19 + t], st[:, 16 + t:17 + t]), waits=[w2])
                w4 = op("vector", lambda e, t=t: e.tensor_tensor(
                    st[:, 18 + t:19 + t], st[:, 18 + t:19 + t],
                    sc_col[:, 3:4], ALU.mult), waits=[w3])
                w5 = op("vector", lambda e, t=t: e.tensor_tensor(
                    st[:, 20 + t:21 + t], st[:, 16 + t:17 + t],
                    sc_col[:, 6:7], ALU.mult), waits=[w2, md3])
                fso.append((w4, w5))

            if DEBUG and q == 0:
                tapm["st"] = dma("gpsimd", taps["t_st"][:, :], st[:],
                                 waits=[fso[0][0], fso[0][1], fso[1][0],
                                        fso[1][1]], sem=PTAP[0])

            r_fm_marks = {}
            rr_last = None
            prev_rtr = None
            first_rfm = True
            for fb in range(16):
                for t in range(T2):
                    sb = (fb * 2 + t) % 2
                    rw = [spq_fin[0], spq_fin[1], fso[t][0]]
                    if sload_free.get(sb) is not None:
                        rw.append(sload_free[sb])
                    mld = dma("gpsimd", sload[:, sb * 512:(sb + 1) * 512],
                              s_s[t * 128:(t + 1) * 128,
                                  fb * 512:(fb + 1) * 512], waits=rw,
                              sem=PSLR[sb])
                    mv1 = op("vector", lambda e, sb=sb, t=t: e.tensor_scalar(
                        v16[:], sload[:, sb * 512:(sb + 1) * 512],
                        st[:, 18 + t:19 + t], 1024.0, ALU.mult, ALU.add),
                        waits=[mld])
                    sload_free[sb] = mv1
                    vw = [mv1]
                    if prev_rtr is not None:
                        vw.append(prev_rtr)
                    mv2 = op("vector", lambda e: e.tensor_scalar(
                        pj_t[:], v16[:], -1024.0, None, ALU.add), waits=vw)
                    for j in range(4):
                        rc = fb * 4 + j
                        pb2 = ps[6] if j % 2 == 0 else ps[7]
                        tw = [mv2]
                        if psum_free.get(id(pb2)) is not None:
                            tw.append(psum_free[id(pb2)])
                        mtr = op("tensor", lambda e, pb2=pb2, j=j: e.transpose(
                            pb2[:, 0:128], pj_t[:, j * 128:(j + 1) * 128],
                            eye_s[:]), waits=tw)
                        prev_rtr = mtr
                        hw = [mtr]
                        if first_rfm:
                            hw.append(m4_last_mm)
                            if tapm.get("y2") is not None:
                                hw.append(tapm["y2"])
                            first_rfm = False
                        mev = op("scalar", lambda e, pb2=pb2, rc=rc, t=t:
                                 e.activation(
                                     out=HSB(rc * LQ + t * 128,
                                             rc * LQ + (t + 1) * 128),
                                     in_=pb2[:, 0:128], func=AF.Copy),
                                 waits=hw)
                        psum_free[id(pb2)] = mev
                        r_fm_marks.setdefault(rc, []).append(mev)
                        rr_last = mev

            if DEBUG and q == 0:
                tapm["rfm"] = dma("gpsimd", taps["t_rfm"][:, :], HSB(0, 16384),
                                  waits=[rr_last], sem=PTAP[0])

            ws_slot_e = {}
            o6_done = [[], []]
            for db in range(4):
                mm_last = {}
                for k in range(NKF):
                    slot = k % 16
                    so = 8192 + slot * 512
                    w = []
                    if first_q and db == 0 and k == 0:
                        pass
                    if db == 0 and k < 16:
                        w.append(m4_last_mm)
                    if ws_slot_e.get(slot) is not None:
                        w.append(ws_slot_e[slot])
                    mq_l = dma("sync", WS[:, so:so + 512],
                               s_q2[k * 128:(k + 1) * 128,
                                    db * 512:(db + 1) * 512], waits=w,
                               sem=PM6[k % 16])
                    for t in range(T2):
                        pb = ps[t]
                        pw = [mq_l, r_fm_marks[k][-1]]
                        if k == 0 and psum_free.get(id(pb)) is not None:
                            pw.append(psum_free[id(pb)])
                        rf = HSB(k * LQ + t * 128, k * LQ + (t + 1) * 128)
                        wq = WS[:, so:so + 512]
                        mm = op("tensor", lambda e, pb=pb, rf=rf, wq=wq,
                                s_=(k == 0), p_=(k == NKF - 1):
                                e.matmul(pb[:, 0:512], rf, wq, start=s_,
                                         stop=p_),
                                waits=pw if (db == 0 or k == 0) else [mq_l])
                        mm_last[t] = mm
                    ws_slot_e[slot] = mm_last[T2 - 1]
                for t in range(T2):
                    pb = ps[t]
                    ew = [mm_last[t], fso[t][1]]
                    if db == 0:
                        ew.append(xq_free)
                    mev = op("vector", lambda e, pb=pb, t=t, db=db:
                             e.tensor_scalar(
                                 xq[:, t * H + db * 512:
                                    t * H + (db + 1) * 512],
                                 pb[:, 0:512], st[:, 20 + t:21 + t], None,
                                 ALU.mult), waits=ew)
                    psum_free[id(pb)] = mev
                    o6_done[t].append(mev)
            m6_last_mm = mm_last[T2 - 1]

            if DEBUG and q == 0:
                tapm["o6"] = dma("gpsimd", taps["t_o6"][:, :], xq[:],
                                 waits=o6_done[0] + o6_done[1], sem=PTAP[0])

            store_marks = []
            for t in range(T2):
                xs = xq[:, t * H:(t + 1) * H]
                lw = list(o6_done[t])
                if t == 1:
                    lw.append(store_marks[0])
                ms = ln_stats(xs, 24, extra_waits=lw)
                ma = op("vector", lambda e, xs=xs: e.tensor_scalar(
                    scr2k[:], xs, st[:, 24:25], st[:, 25:26], ALU.subtract,
                    ALU.mult), waits=[ms])
                mw_ = [ma]
                if prev_store is not None and t == 0:
                    mw_.append(prev_store)
                mr_ = op("vector", lambda e, t=t: e.tensor_tensor(
                    out_t[:, 0:H], scr2k[:], x2[:, t * H:(t + 1) * H],
                    ALU.add), waits=mw_)
                mst = dma("gpsimd", dout[lo + t * 128:lo + (t + 1) * 128, :],
                          out_t[:, 0:H], waits=[mr_], sem=PST[t])
                store_marks.append(mst)
            prev_store = store_marks[-1]
            out_dma_marks = list(store_marks)
            ws_free = m6_last_mm
            hs_free = m6_last_mm
            x1_free = m5_last_mm
            first_q = False

        prog.emit("vector", lambda e: e.engine_nop(),
                  waits=list(out_dma_marks))

        @block.sync
        def _(sync):
            prog.replay("sync", sync)

        @block.vector
        def _(vector):
            prog.replay("vector", vector)

        @block.scalar
        def _(scalar):
            prog.replay("scalar", scalar)

        @block.tensor
        def _(tensor):
            prog.replay("tensor", tensor)

        @block.gpsimd
        def _(gpsimd):
            prog.replay("gpsimd", gpsimd)

    return nc


def kernel(L0_attn_x, L0_attn_Win, L0_attn_Wconv, L0_attn_bconv, L0_attn_Wx,
           L0_attn_Wdt, L0_attn_bdt, L0_attn_Wout, L0_attn_A_log, L0_attn_D,
           L0_ffn_c_fc, L0_ffn_c_proj):
    x = np.ascontiguousarray(np.asarray(L0_attn_x, dtype=np.float32))
    winT = np.ascontiguousarray(np.asarray(L0_attn_Win, np.float32).T)
    wconv2 = np.ascontiguousarray(np.asarray(L0_attn_Wconv, np.float32)
                                  .reshape(NKD, 128, K4).transpose(1, 0, 2)
                                  .reshape(128, NKD * K4))
    bconv2 = np.ascontiguousarray(np.asarray(L0_attn_bconv, np.float32)
                                  .reshape(NKD, 128).T)
    wx2 = np.ascontiguousarray(np.asarray(L0_attn_Wx, np.float32).T
                               .reshape(NKD, 128, 160).transpose(1, 0, 2)
                               .reshape(128, NKD * 160))
    wdtT = np.ascontiguousarray(np.asarray(L0_attn_Wdt, np.float32).T)
    bdt2 = np.ascontiguousarray(np.asarray(L0_attn_bdt, np.float32)
                                .reshape(NKD, 128).T)
    woutT = np.ascontiguousarray(np.asarray(L0_attn_Wout, np.float32).T)
    alog2 = np.ascontiguousarray(np.asarray(L0_attn_A_log, np.float32)
                                 .reshape(NKD, 128, N16).transpose(1, 0, 2)
                                 .reshape(128, NKD * N16))
    dvec2 = np.ascontiguousarray(np.asarray(L0_attn_D, np.float32)
                                 .reshape(NKD, 128).T)
    cfc = np.ascontiguousarray(np.asarray(L0_ffn_c_fc, np.float32))
    cprojT = np.ascontiguousarray(np.asarray(L0_ffn_c_proj, np.float32).T)
    eye = np.eye(128, dtype=np.float32)

    nc = build_nc()
    in_maps = []
    for c in range(8):
        in_maps.append(dict(x=np.ascontiguousarray(x[c]), winT=winT,
                            wconv2=wconv2, bconv2=bconv2, wx2=wx2, wdtT=wdtT,
                            bdt2=bdt2, woutT=woutT, alog2=alog2, dvec2=dvec2,
                            cfc=cfc, cprojT=cprojT, eye=eye))
    res = run_bass_kernel_spmd(nc, in_maps, core_ids=list(range(8)))
    if DEBUG:
        kernel._last_res = res
    return np.stack([r["out"] for r in res.results], axis=0)


# version 05
# speedup vs baseline: 1.1078x; 1.0966x over previous
import contextlib
import numpy as np
import concourse.bass as bass
import concourse.mybir as mybir
from concourse.bass_utils import run_bass_kernel_spmd

F32 = mybir.dt.float32
F16 = mybir.dt.float16
BF16 = mybir.dt.bfloat16
AF = mybir.ActivationFunctionType
ALU = mybir.AluOpType
AXX = mybir.AxisListType.X

H = 2048
DI = 4096
E2 = 2 * DI
N16 = 16
K4 = 4
R = 128
L = 2048
HF = 8192
NQ = 8
LQ = 256
T2 = 2
NKH = 16
NKD = 32
NKF = 64
EPS = 1e-5
DEBUG = False


class Sem:
    def __init__(self, handle):
        self.h = handle
        self.count = 0


class Prog:
    ENGS = ("sync", "vector", "scalar", "tensor", "gpsimd")

    def __init__(self):
        self.ops = {e: [] for e in self.ENGS}

    def emit(self, eng, builder, waits=(), inc=None):
        for s, v in waits:
            assert v <= s.count, f"wait {v} > scheduled {s.count} on {eng}"
        if inc is not None:
            inc[0].count += inc[1]
            incv = (inc[0].h, inc[1])
        else:
            incv = None
        self.ops[eng].append((list(waits), builder, incv))

    def replay(self, eng_name, eng):
        for waits, builder, incv in self.ops[eng_name]:
            for s, v in waits:
                if v > 0:
                    eng.wait_ge(s.h, v)
            ins = builder(eng)
            if incv is not None:
                ins.then_inc(incv[0], incv[1])


def build_nc():
    nc = bass.Bass("TRN2", target_bir_lowering=False, debug=False)

    dx = nc.dram_tensor("x", [L, H], F32, kind="ExternalInput").ap()
    dwinT = nc.dram_tensor("winT", [H, E2], F32, kind="ExternalInput").ap()
    dwconv = nc.dram_tensor("wconv2", [128, NKD * K4], F32, kind="ExternalInput").ap()
    dbconv = nc.dram_tensor("bconv2", [128, NKD], F32, kind="ExternalInput").ap()
    dwx = nc.dram_tensor("wx2", [128, NKD * 160], F32, kind="ExternalInput").ap()
    dwdtT = nc.dram_tensor("wdtT", [R, DI], F32, kind="ExternalInput").ap()
    dbdt = nc.dram_tensor("bdt2", [128, NKD], F32, kind="ExternalInput").ap()
    dwoutT = nc.dram_tensor("woutT", [DI, H], F32, kind="ExternalInput").ap()
    dalog = nc.dram_tensor("alog2", [128, NKD * N16], F32, kind="ExternalInput").ap()
    ddvec = nc.dram_tensor("dvec2", [128, NKD], F32, kind="ExternalInput").ap()
    dcfc = nc.dram_tensor("cfc", [H, HF], F32, kind="ExternalInput").ap()
    dcprojT = nc.dram_tensor("cprojT", [HF, H], F32, kind="ExternalInput").ap()
    deye = nc.dram_tensor("eye", [128, 128], F32, kind="ExternalInput").ap()
    dout = nc.dram_tensor("out", [L, H], F32, kind="ExternalOutput").ap()

    s_win_h = nc.dram_tensor("s_win_h", [H, E2], BF16).ap()
    s_win_l = nc.dram_tensor("s_win_l", [H, E2], BF16).ap()
    s_wout_h = nc.dram_tensor("s_wout_h", [DI, H], BF16).ap()
    s_wout_l = nc.dram_tensor("s_wout_l", [DI, H], BF16).ap()
    s_q1 = nc.dram_tensor("s_q1", [H, HF], BF16,
                          kind="ExternalOutput" if DEBUG else "Internal").ap()
    s_q2 = nc.dram_tensor("s_q2", [HF, H], BF16,
                          kind="ExternalOutput" if DEBUG else "Internal").ap()
    s_sg = nc.dram_tensor("s_sg", [DI, L], F32).ap()
    s_s = nc.dram_tensor("s_s", [LQ, HF], F32).ap()
    s_bc = nc.dram_tensor("s_bc", [32, LQ], F32).ap()
    s_sc = nc.dram_tensor("s_sc", [1, 8], F32).ap()

    taps = {}
    if DEBUG:
        taps["t_hs"] = nc.dram_tensor("t_hs", [128, 8192], F32, kind="ExternalOutput").ap()
        taps["t_dtin"] = nc.dram_tensor("t_dtin", [128, 2 * LQ], BF16, kind="ExternalOutput").ap()
        taps["t_y2"] = nc.dram_tensor("t_y2", [128, 16384], BF16, kind="ExternalOutput").ap()
        taps["t_mo"] = nc.dram_tensor("t_mo", [128, 4096], F32, kind="ExternalOutput").ap()
        taps["t_x2"] = nc.dram_tensor("t_x2", [128, 4096], F32, kind="ExternalOutput").ap()
        taps["t_rfm"] = nc.dram_tensor("t_rfm", [128, 16384], BF16, kind="ExternalOutput").ap()
        taps["t_hst"] = nc.dram_tensor("t_hst", [128, 512], F32, kind="ExternalOutput").ap()
        taps["t_sc"] = nc.dram_tensor("t_sc", [128, 8], F32, kind="ExternalOutput").ap()
        taps["t_st"] = nc.dram_tensor("t_st", [128, 32], F32, kind="ExternalOutput").ap()
        taps["t_o6"] = nc.dram_tensor("t_o6", [128, 4096], F32, kind="ExternalOutput").ap()

    A = nc.alloc_sbuf_tensor
    eye_s = A("eye_s", [128, 128], F32)
    wconv_c = A("wconv_c", [128, NKD * K4], F32)
    bconv_c = A("bconv_c", [128, NKD], F32)
    bdt_c = A("bdt_c", [128, NKD], F32)
    dvec_c = A("dvec_c", [128, NKD], F32)
    nA_c = A("nA_c", [128, NKD * N16], F32)
    sc_col = A("sc_col", [128, 8], F32)
    st = A("st", [128, 32], F32)
    accum = None
    wx_hi = A("wx_hi", [128, NKD * 160], BF16)
    wx_lo = A("wx_lo", [128, NKD * 160], BF16)
    wdt_hi = A("wdt_hi", [128, DI], BF16)
    wdt_lo = A("wdt_lo", [128, DI], BF16)
    hstate = A("hstate", [128, NKD * N16], F32)
    hprev = A("hprev", [128, NKD * 3], F32)
    scr2k = A("scr2k", [128, H], F32)
    out_t = A("out_t", [128, H], F32)
    xq = A("xq", [128, T2 * H], F32)
    x1f = A("x1f", [128, 2 * NKH * LQ], BF16)
    WS = A("WS", [128, 16384], BF16)
    hs = A("hs", [128, NKD * LQ], F32)
    hsx = A("hsx", [128, LQ + 4], F32)
    hs_h = A("hs_h", [128, LQ], BF16)
    hs_l = A("hs_l", [128, LQ], BF16)
    cvt = A("cvt", [128, LQ], F32)
    sgt = A("sgt", [128, 2 * 128], F32)
    sg_c = A("sg_c", [128, 2 * LQ], F32)
    dtin_h = A("dtin_h", [128, LQ], BF16)
    dtin_l = A("dtin_l", [128, LQ], BF16)
    dt_t = A("dt_t", [128, LQ], F32)
    u_t = A("u_t", [128, LQ], F32)
    GW = 4 * (LQ + 1)
    a8 = A("a8", [128, GW], F32)
    b8 = A("b8", [128, GW], F32)
    h8 = A("h8", [128, GW], F32)
    ym = A("ym", [128, 4 * LQ], F32)
    bc_t = ym[:, 0:LQ]
    yt2 = ym[:, 3 * LQ:4 * LQ]
    y_t = A("y_t", [128, LQ], F32)
    pj_t = A("pj_t", [128, 512], F32)
    sq_t = A("sq_t", [128, 2 * 512], F32)
    sload = A("sload", [128, 2 * 512], F32)
    v16 = A("v16", [128, 512], F16)
    x2 = A("x2", [128, T2 * H], F32)
    accum = sq_t[:, 0:96]
    rg_t = pj_t[:, 0:512]
    csts = A("csts", [128, 2], F32)

    def MO(a, b):
        return x1f[:, 2 * a:2 * b].bitcast(F32)

    def BB(a, b):
        return WS[:, 2 * a:2 * b].bitcast(F32)

    def BBS(n):
        if n < 8:
            return out_t[:, n * LQ:(n + 1) * LQ]
        if n < 12:
            return sq_t[:, (n - 8) * LQ:(n - 7) * LQ]
        return sload[:, (n - 12) * LQ:(n - 11) * LQ]

    def CBS(n):
        return x2[:, n * LQ:(n + 1) * LQ]

    def BBG(g):
        if g < 2:
            return out_t[:, g * 4 * LQ:(g + 1) * 4 * LQ].rearrange(
                "p (n l) -> p n l", n=4)
        if g == 2:
            return sq_t[:, 0:4 * LQ].rearrange("p (n l) -> p n l", n=4)
        return sload[:, 0:4 * LQ].rearrange("p (n l) -> p n l", n=4)

    def CBG(g):
        return x2[:, g * 4 * LQ:(g + 1) * 4 * LQ].rearrange(
            "p (n l) -> p n l", n=4)

    def HSB(a, b):
        return hs[:, a // 2:b // 2].bitcast(BF16)

    def WSF16(a, b):
        return WS[:, a:b].bitcast(F16)

    P = nc.alloc_psum_tensor
    ps = [P(f"ps{i}", [128, 512], F32) for i in range(8)]

    prog = Prog()
    with contextlib.ExitStack() as stack:
        def mksem(name):
            return Sem(stack.enter_context(nc.semaphore(name)))

        def mkpool(name, n):
            return [mksem(f"{name}{i}") for i in range(n)]

        qs, qg, qc = mksem("q_sync"), mksem("q_gp"), mksem("q_sc")
        ev, es, et = mksem("e_v"), mksem("e_s"), mksem("e_t")
        PWH = mkpool("pwh", 16)
        PM6 = mkpool("pm6", 16)
        PWL = mkpool("pwl", 16)
        PP = mkpool("pp", 2)
        PPW = mkpool("ppw", 2)
        PXL = mkpool("pxl", 2)
        PBB = mkpool("pbb", 1)
        PBN = mkpool("pbn", 1)
        PSM = mkpool("psm", 1)
        PSGR = mkpool("psgr", 2)
        PSLR = mkpool("pslr", 2)
        PSPG = mkpool("pspg", 2)
        PSPQ = mkpool("pspq", 2)
        PST = mkpool("pst", 2)
        PTAP = mkpool("ptap", 1)
        block = stack.enter_context(nc.Block())

        DMAQ = {"sync": qs, "gpsimd": qg, "scalar": qc}
        EPROG = {"vector": ev, "scalar": es, "tensor": et}

        def dma(queue, out, in_, waits=(), sem=None):
            q = sem if sem is not None else DMAQ[queue]
            prog.emit(queue, lambda e, o=out, i=in_: e.dma_start(out=o, in_=i),
                      waits=waits, inc=(q, 16))
            return (q, q.count)

        def op(eng, builder, waits=()):
            p = EPROG[eng]
            prog.emit(eng, builder, waits=waits, inc=(p, 1))
            return (p, p.count)

        dma("sync", eye_s[:], deye[:, :], sem=PSM[0])
        dma("sync", wconv_c[:], dwconv[:, :], sem=PSM[0])
        dma("sync", bconv_c[:], dbconv[:, :], sem=PSM[0])
        dma("sync", bdt_c[:], dbdt[:, :], sem=PSM[0])
        dma("sync", dvec_c[:], ddvec[:, :], sem=PSM[0])
        m_small = dma("sync", nA_c[:], dalog[:, :], sem=PSM[0])
        m_eye = m_wc = m_bc = m_bd = m_dv = m_al = m_small
        m1 = op("scalar", lambda e: e.activation(out=nA_c[:], in_=nA_c[:],
                                                 func=AF.Exp), waits=[m_al])
        m_nA = op("vector", lambda e: e.tensor_scalar_mul(nA_c[:], nA_c[:], -1.0),
                  waits=[m1])
        m_hs0 = op("vector", lambda e: e.memset(hstate[:], 0.0))
        m_hp0 = op("vector", lambda e: e.memset(hprev[:], 0.0))
        op("vector", lambda e: e.memset(csts[:, 0:1], EPS))
        m_cst = op("vector", lambda e: e.memset(csts[:, 1:2], 1.0))
        m_a80 = op("vector", lambda e: e.memset(
            a8[:].rearrange("p (g w) -> p g w", g=4)[:, :, 0:1], 0.0))

        m_wx = dma("sync", hs[:, 0:NKD * 160], dwx[:, :], sem=PP[0])
        m_wxh = op("scalar", lambda e: e.copy(wx_hi[:], hs[:, 0:NKD * 160]),
                   waits=[m_wx])
        m_wxl = op("vector", lambda e: e.tensor_tensor(
            wx_lo[:], hs[:, 0:NKD * 160], wx_hi[:], ALU.subtract), waits=[m_wxh])
        m_wdt = dma("sync", hs[:, 0:DI], dwdtT[:, :], waits=[m_wxh, m_wxl],
                    sem=PP[1])
        m_wdh = op("scalar", lambda e: e.copy(wdt_hi[:], hs[:, 0:DI]),
                   waits=[m_wdt])
        m_wdl = op("vector", lambda e: e.tensor_tensor(
            wdt_lo[:], hs[:, 0:DI], wdt_hi[:], ALU.subtract),
            waits=[m_wdh])

        last_v, last_s, last_wr = m_wxl, m_wdh, None

        def split_weight(dsrc, nk, ncols, dsth, dstl):
            nonlocal last_v, last_s, last_wr
            for k in range(nk):
                w = [last_v, last_s, m_wdl]
                if last_wr is not None:
                    w.append(last_wr)
                mld = dma("sync", hs[:, 0:ncols], dsrc[k * 128:(k + 1) * 128, :],
                          waits=w, sem=PP[k % 2])
                mh = op("scalar", lambda e, n_=ncols: e.copy(
                    WS[:, 0:n_], hs[:, 0:n_]), waits=[mld])
                ml = op("vector", lambda e, n_=ncols: e.tensor_tensor(
                    WS[:, 8192:8192 + n_], hs[:, 0:n_], WS[:, 0:n_],
                    ALU.subtract), waits=[mh])
                dma("gpsimd", dsth[k * 128:(k + 1) * 128, :], WS[:, 0:ncols],
                    waits=[mh], sem=PPW[0])
                mw2 = dma("gpsimd", dstl[k * 128:(k + 1) * 128, :],
                          WS[:, 8192:8192 + ncols], waits=[ml], sem=PPW[1])
                last_v, last_s, last_wr = ml, mh, mw2

        split_weight(dwinT, NKH, E2, s_win_h, s_win_l)
        split_weight(dwoutT, NKD, H, s_wout_h, s_wout_l)

        def ternary_prep(dsrc, nk, ncols, dstq, acc0, sc_i):
            nonlocal last_v, last_s, last_wr
            pass1 = []
            for k in range(nk):
                w = [last_v, last_s]
                if last_wr is not None:
                    w.append(last_wr)
                mld = dma("sync", hs[:, 0:ncols], dsrc[k * 128:(k + 1) * 128, :],
                          waits=w, sem=PP[k % 2])
                ma = op("scalar", lambda e, n_=ncols, k=k: e.activation(
                    out=BB(0, n_), in_=hs[:, 0:n_], func=AF.Abs,
                    accum_out=accum[:, acc0 + k:acc0 + k + 1]), waits=[mld])
                last_v, last_s, last_wr = ma, ma, None
                pass1.append(ma)
            mr = op("vector", lambda e: e.reduce_sum(
                accum[:, 88:89], accum[:, acc0:acc0 + nk], axis=AXX),
                waits=[pass1[-1]])
            mo_ = op("vector", lambda e: e.memset(accum[:, 89:90], 1.0))
            mps = op("tensor", lambda e: e.matmul(
                ps[6][0:1, 0:1], accum[:, 89:90], accum[:, 88:89],
                start=True, stop=True), waits=[mr, mo_])
            mv = op("vector", lambda e, nn=nk * 128 * ncols: e.tensor_scalar(
                accum[0:1, 90:91], ps[6][0:1, 0:1], 1.0 / nn, EPS,
                ALU.mult, ALU.max), waits=[mps])
            mb = dma("gpsimd", s_sc[0:1, sc_i:sc_i + 1], accum[0:1, 90:91],
                     waits=[mv], sem=PBN[0])
            mbc = dma("gpsimd", sc_col[:, sc_i:sc_i + 1],
                      s_sc[0:1, sc_i:sc_i + 1].partition_broadcast(128),
                      waits=[mb], sem=PBN[0])
            minv = op("vector", lambda e, i=sc_i: e.reciprocal(
                sc_col[:, i + 4:i + 5], sc_col[:, i:i + 1]), waits=[mbc])
            for k in range(nk):
                w = [last_v, last_s, minv]
                if last_wr is not None:
                    w.append(last_wr)
                mld = dma("sync", hs[:, 0:ncols], dsrc[k * 128:(k + 1) * 128, :],
                          waits=w, sem=PP[k % 2])
                mq0 = op("vector", lambda e, n_=ncols, i=sc_i: e.tensor_scalar(
                    hs[:, 0:n_], hs[:, 0:n_], sc_col[:, i + 4:i + 5], None,
                    ALU.mult), waits=[mld])
                mq1 = op("vector", lambda e, n_=ncols: e.tensor_scalar(
                    WSF16(0, n_), hs[:, 0:n_], 0.5, None, ALU.is_gt),
                    waits=[mq0])
                mq2 = op("vector", lambda e, n_=ncols: e.tensor_scalar(
                    hs[:, 0:n_], hs[:, 0:n_], -0.5, None, ALU.is_lt),
                    waits=[mq1])
                mq3 = op("vector", lambda e, n_=ncols: e.tensor_tensor(
                    WS[:, 8192:8192 + n_], WSF16(0, n_), hs[:, 0:n_],
                    ALU.subtract), waits=[mq2])
                mwq = dma("gpsimd", dstq[k * 128:(k + 1) * 128, :],
                          WS[:, 8192:8192 + ncols], waits=[mq3], sem=PPW[k % 2])
                last_v, last_s, last_wr = mq3, mq3, mwq
            return mbc

        mbc1 = ternary_prep(dcfc, NKH, HF, s_q1, 0, 0)
        mbc2 = ternary_prep(dcprojT, NKF, H, s_q2, 16, 1)
        prep_last = [(PPW[0], PPW[0].count), (PPW[1], PPW[1].count)]
        md1 = op("vector", lambda e: e.tensor_tensor(
            sc_col[:, 2:3], sc_col[:, 0:1], sc_col[:, 0:1], ALU.mult),
            waits=[mbc1])
        md2 = op("vector", lambda e: e.tensor_scalar(
            sc_col[:, 3:4], sc_col[:, 2:3], 255.0, None, ALU.mult), waits=[md1])
        md3 = op("vector", lambda e: e.tensor_scalar(
            sc_col[:, 6:7], sc_col[:, 1:2], 1.0 / 255.0, None, ALU.mult),
            waits=[mbc2])
        tapm = {}
        if DEBUG:
            tapm["sc"] = dma("gpsimd", taps["t_sc"][:, :], sc_col[:],
                             waits=[md1, md2, md3], sem=PTAP[0])

        def ln_stats(xsrc, si, extra_waits=()):
            w = list(extra_waits)
            ms1 = op("vector", lambda e: e.reduce_sum(st[:, si:si + 1], xsrc,
                                                      axis=AXX), waits=w)
            ms2 = op("scalar", lambda e: e.activation(
                out=out_t[:, 0:H], in_=xsrc, func=AF.Square,
                accum_out=st[:, si + 1:si + 2]), waits=w)
            ms3 = op("vector", lambda e: e.tensor_scalar_mul(
                st[:, si:si + 1], st[:, si:si + 1], 1.0 / H), waits=[ms1])
            ms4 = op("vector", lambda e: e.tensor_scalar_mul(
                st[:, si + 1:si + 2], st[:, si + 1:si + 2], 1.0 / H),
                waits=[ms2])
            ms5 = op("vector", lambda e: e.tensor_tensor(
                st[:, si + 2:si + 3], st[:, si:si + 1], st[:, si:si + 1],
                ALU.mult), waits=[ms3])
            ms6 = op("vector", lambda e: e.tensor_tensor(
                st[:, si + 1:si + 2], st[:, si + 1:si + 2],
                st[:, si + 2:si + 3], ALU.subtract), waits=[ms4, ms5])
            ms7 = op("scalar", lambda e: e.activation(
                out=st[:, si + 1:si + 2], in_=st[:, si + 1:si + 2],
                func=AF.Sqrt, bias=csts[:, 0:1]), waits=[ms6, m_cst])
            ms8 = op("vector", lambda e: e.reciprocal(
                st[:, si + 1:si + 2], st[:, si + 1:si + 2]), waits=[ms7])
            return ms8

        psum_free = {}

        def fm_split(t, waits_in):
            ml = None
            for hc in range(NKH):
                pb = ps[6] if hc % 2 == 0 else ps[7]
                tw = list(waits_in) + [m_eye]
                if psum_free.get(id(pb)) is not None:
                    tw.append(psum_free[id(pb)])
                mtr = op("tensor", lambda e, pb=pb, hc=hc: e.transpose(
                    pb[:, 0:128], scr2k[:, hc * 128:(hc + 1) * 128], eye_s[:]),
                    waits=tw)
                dh = x1f[:, hc * LQ + t * 128: hc * LQ + (t + 1) * 128]
                dl = x1f[:, 4096 + hc * LQ + t * 128: 4096 + hc * LQ + (t + 1) * 128]
                mh = op("scalar", lambda e, pb=pb, d=dh: e.activation(
                    out=d, in_=pb[:, 0:128], func=AF.Copy), waits=[mtr])
                ml = op("vector", lambda e, pb=pb, d=dl, s_=dh: e.tensor_tensor(
                    d, pb[:, 0:128], s_, ALU.subtract), waits=[mh])
                psum_free[id(pb)] = ml
            return ml

        xq_free = None
        hs_free = None
        ws_free = None
        x1_free = None
        sgt_free = {}
        sgc_free = {}
        sq_free = {}
        sload_free = {}
        out_dma_marks = [None, None]
        prev_store = None
        first_q = True

        for q in range(NQ):
            lo = q * LQ
            mx = []
            for t in range(T2):
                w = []
                if xq_free is not None:
                    w.append(xq_free)
                if prev_store is not None:
                    w.append(prev_store)
                mx.append(dma("gpsimd", xq[:, t * H:(t + 1) * H],
                              dx[lo + t * 128:lo + (t + 1) * 128, :], waits=w,
                              sem=PXL[t]))
            m_x1 = []
            prev_split = None
            for t in range(T2):
                xs = xq[:, t * H:(t + 1) * H]
                lw = [mx[t]]
                if out_dma_marks[t] is not None:
                    lw.append(out_dma_marks[t])
                ms = ln_stats(xs, 0, extra_waits=lw)
                maw = [ms]
                if prev_split is not None:
                    maw.append(prev_split)
                if x1_free is not None and t == 0:
                    maw.append(x1_free)
                ma = op("vector", lambda e, xs=xs: e.tensor_scalar(
                    scr2k[:], xs, st[:, 0:1], st[:, 1:2], ALU.subtract,
                    ALU.mult), waits=maw)
                prev_split = fm_split(t, [ma])
                m_x1.append(prev_split)

            ws_slot = {}
            conv_marks = {}
            gate_spill = {}
            hs_piece = {}
            m2_last = None
            hs_gate_done = None
            last_m1_mm = None

            def m1_eblock(eb):
                nonlocal last_m1_mm
                mm_last = {}
                for k in range(NKH):
                    so = k * 1024
                    w = []
                    if first_q and eb == 0 and k == 0:
                        w += prep_last
                    if eb == 0 and k == 0 and ws_free is not None:
                        w.append(ws_free)
                    if ws_slot.get(k) is not None:
                        w.append(ws_slot[k])
                    mh_l = dma("sync", WS[:, so:so + 512],
                               s_win_h[k * 128:(k + 1) * 128,
                                       eb * 512:(eb + 1) * 512], waits=w,
                               sem=PWH[k])
                    ml_l = dma("scalar", WS[:, so + 512:so + 1024],
                               s_win_l[k * 128:(k + 1) * 128,
                                       eb * 512:(eb + 1) * 512], waits=w,
                               sem=PWL[k])
                    for t in range(T2):
                        pb = ps[t]
                        pw = [mh_l, ml_l, m_x1[t]]
                        if k == 0 and psum_free.get(id(pb)) is not None:
                            pw.append(psum_free[id(pb)])
                        xh = x1f[:, k * LQ + t * 128: k * LQ + (t + 1) * 128]
                        xl = x1f[:, 4096 + k * LQ + t * 128:
                                 4096 + k * LQ + (t + 1) * 128]
                        wh = WS[:, so:so + 512]
                        wl = WS[:, so + 512:so + 1024]
                        op("tensor", lambda e, pb=pb, xh=xh, wh=wh, s_=(k == 0):
                           e.matmul(pb[:, 0:512], xh, wh, start=s_, stop=False),
                           waits=pw if k == 0 else [mh_l, ml_l])
                        op("tensor", lambda e, pb=pb, xh=xh, wl=wl:
                           e.matmul(pb[:, 0:512], xh, wl, start=False,
                                    stop=False))
                        mm = op("tensor", lambda e, pb=pb, xl=xl, wh=wh,
                                p_=(k == NKH - 1):
                                e.matmul(pb[:, 0:512], xl, wh, start=False,
                                         stop=p_))
                        mm_last[t] = mm
                    ws_slot[k] = mm_last[T2 - 1]
                    last_m1_mm = mm_last[T2 - 1]
                return mm_last

            def m1_evac(eb, t, mm_t, prev_tr):
                ew_ = [mm_t]
                if prev_tr is not None:
                    ew_.append(prev_tr)
                me_ = op("scalar", lambda e, pb=ps[t]: e.activation(
                    out=pj_t[:], in_=pb[:, 0:512], func=AF.Copy), waits=ew_)
                psum_free[id(ps[t])] = me_
                last_tr = None
                for j in range(4):
                    c = eb * 4 + j
                    pb2 = ps[6] if j % 2 == 0 else ps[7]
                    tw = [me_]
                    if psum_free.get(id(pb2)) is not None:
                        tw.append(psum_free[id(pb2)])
                    mtr = op("tensor", lambda e, pb2=pb2, j=j: e.transpose(
                        pb2[:, 0:128], pj_t[:, j * 128:(j + 1) * 128], eye_s[:]),
                        waits=tw)
                    last_tr = mtr
                    if c < NKD:
                        hw = [mtr]
                        if hs_free is not None and eb == 0 and j == 0 and t == 0:
                            hw.append(hs_free)
                            if tapm.get("rfm") is not None:
                                hw.append(tapm["rfm"])
                        mev = op("scalar", lambda e, pb2=pb2, c=c, t=t:
                                 e.copy(hs[:, c * LQ + t * 128:
                                           c * LQ + (t + 1) * 128],
                                        pb2[:, 0:128]), waits=hw)
                        hs_piece.setdefault(c, []).append(mev)
                    else:
                        g = c - NKD
                        sb = (g * 2 + t) % 2
                        gw = [mtr]
                        if sgt_free.get(sb) is not None:
                            gw.append(sgt_free[sb])
                        mev = op("scalar", lambda e, pb2=pb2, sb=sb: e.activation(
                            out=sgt[:, sb * 128:(sb + 1) * 128],
                            in_=pb2[:, 0:128], func=AF.Silu), waits=gw)
                        sgt_free[sb] = dma(
                            "gpsimd",
                            s_sg[g * 128:(g + 1) * 128,
                                 lo + t * 128:lo + (t + 1) * 128],
                            sgt[:, sb * 128:(sb + 1) * 128], waits=[mev],
                            sem=PSPG[sb])
                        gate_spill[(g, t)] = sgt_free[sb]
                    psum_free[id(pb2)] = mev
                return last_tr

            prev_tr = None
            for eb in range(8):
                mm_last = m1_eblock(eb)
                for t in range(T2):
                    prev_tr = m1_evac(eb, t, mm_last[t], prev_tr)
                for j in range(4):
                    c = eb * 4 + j
                    pw = list(hs_piece[c])
                    mh1 = op("scalar", lambda e, c=c: e.copy(
                        hsx[:, 0:3], hprev[:, c * 3:(c + 1) * 3]),
                        waits=pw + [m_hp0])
                    mh2 = op("scalar", lambda e, c=c: e.copy(
                        hsx[:, 3:LQ + 3], hs[:, c * LQ:(c + 1) * LQ]), waits=pw)
                    mh3 = op("scalar", lambda e, c=c: e.copy(
                        hprev[:, c * 3:(c + 1) * 3], hsx[:, LQ:LQ + 3]),
                        waits=[mh2])
                    mc = op("vector", lambda e, c=c: e.tensor_scalar(
                        cvt[:], hsx[:, 0:LQ], wconv_c[:, c * K4:c * K4 + 1],
                        None, ALU.mult), waits=[mh1, mh2, m_wc])
                    for kk in range(1, K4):
                        mc = op("vector", lambda e, c=c, kk=kk:
                                e.scalar_tensor_tensor(
                                    cvt[:], hsx[:, kk:kk + LQ],
                                    wconv_c[:, c * K4 + kk:c * K4 + kk + 1],
                                    cvt[:], ALU.mult, ALU.add), waits=[mc])
                    mhs = op("scalar", lambda e, c=c: e.activation(
                        out=hs[:, c * LQ:(c + 1) * LQ], in_=cvt[:],
                        func=AF.Silu, bias=bconv_c[:, c:c + 1]),
                        waits=[mc, m_bc, mh3])
                    conv_marks[c] = mhs
                    hhw = [mhs]
                    if m2_last is not None:
                        hhw.append(m2_last)
                    mhh = op("scalar", lambda e, c=c: e.copy(
                        hs_h[:], hs[:, c * LQ:(c + 1) * LQ]), waits=hhw)
                    mhl = op("vector", lambda e, c=c: e.tensor_tensor(
                        hs_l[:], hs[:, c * LQ:(c + 1) * LQ], hs_h[:],
                        ALU.subtract), waits=[mhh])
                    m2w = [mhh, mhl, m_wxl]
                    if c == 0:
                        for b in (2, 3):
                            if psum_free.get(id(ps[b])) is not None:
                                m2w.append(psum_free[id(ps[b])])
                    for pi, (wt, xt) in enumerate(
                            [(wx_hi, hs_h), (wx_hi, hs_l), (wx_lo, hs_h)]):
                        s_ = (c == 0 and pi == 0)
                        p_ = (c == NKD - 1 and pi == 2)
                        op("tensor", lambda e, w=wt[:, c * 160:c * 160 + 128],
                           xt=xt, s_=s_, p_=p_:
                           e.matmul(ps[2][:, 0:LQ], w, xt[:], start=s_,
                                    stop=p_), waits=m2w if pi == 0 else [])
                        mmb = op("tensor",
                                 lambda e, w=wt[:, c * 160 + 128:(c + 1) * 160],
                                 xt=xt, s_=s_, p_=p_:
                                 e.matmul(ps[3][0:32, 0:LQ], w, xt[:],
                                          start=s_, stop=p_))
                    m2_last = mmb

            for eb in range(8, 16):
                mm_last = m1_eblock(eb)
                for t in range(T2):
                    prev_tr = m1_evac(eb, t, mm_last[t], prev_tr)
            spg_fin = (sgt_free[0], sgt_free[1])

            mdh = op("scalar", lambda e: e.activation(
                out=dtin_h[:], in_=ps[2][:, 0:LQ], func=AF.Copy),
                waits=[m2_last])
            mdl = op("vector", lambda e: e.tensor_tensor(
                dtin_l[:], ps[2][:, 0:LQ], dtin_h[:], ALU.subtract),
                waits=[mdh])
            psum_free[id(ps[2])] = mdl
            mbce = op("vector", lambda e: e.tensor_copy(
                bc_t[0:32, 0:LQ], ps[3][0:32, 0:LQ]), waits=[m2_last])
            psum_free[id(ps[3])] = mbce
            mbcd = dma("gpsimd", s_bc[:, :], bc_t[0:32, 0:LQ], waits=[mbce],
                       sem=PBN[0])
            bb_w = [mbcd]
            for t_ in range(T2):
                if out_dma_marks[t_] is not None:
                    bb_w.append(out_dma_marks[t_])
            for sb_ in (0, 1):
                if sq_free.get(sb_) is not None:
                    bb_w.append(sq_free[sb_])
                if sload_free.get(sb_) is not None:
                    bb_w.append(sload_free[sb_])
            if prev_store is not None:
                bb_w.append(prev_store)
            m_bb = None
            for n in range(N16):
                dma("gpsimd", BBS(n),
                    s_bc[n:n + 1, :].partition_broadcast(128), waits=bb_w,
                    sem=PBB[0])
                m_bb = dma("gpsimd", CBS(n),
                           s_bc[N16 + n:N16 + n + 1, :].partition_broadcast(128),
                           waits=[mbcd], sem=PBB[0])

            if DEBUG and q == 0:
                tapm["hs"] = dma("gpsimd", taps["t_hs"][:, :], hs[:, 0:8192],
                                 waits=[conv_marks[c] for c in range(NKD)],
                                 sem=PTAP[0])
                dma("gpsimd", taps["t_dtin"][:, 0:LQ], dtin_h[:], waits=[mdl],
                    sem=PTAP[0])
                tapm["dtin"] = dma("gpsimd", taps["t_dtin"][:, LQ:2 * LQ],
                                   dtin_l[:], waits=[mdl], sem=PTAP[0])

            y2_marks = []
            scan_last = None
            prev_mu = None
            prev_m2h = None
            prev_scan_g = {}
            for c in range(NKD):
                pb = ps[4 + c % 2]
                m3w = [mdh, mdl, m_wdl]
                if psum_free.get(id(pb)) is not None:
                    m3w.append(psum_free[id(pb)])
                mm3 = None
                for pi, (wt_, xt_) in enumerate([(wdt_hi, dtin_h),
                                                 (wdt_hi, dtin_l),
                                                 (wdt_lo, dtin_h)]):
                    mm3 = op("tensor", lambda e, pb=pb,
                             w=wt_[:, c * 128:(c + 1) * 128], x=xt_,
                             s_=(pi == 0), p_=(pi == 2):
                             e.matmul(pb[:, 0:LQ], w, x[:], start=s_, stop=p_),
                             waits=m3w if pi == 0 else [])
                s1w = [mm3, m_bd]
                if prev_mu is not None:
                    s1w.append(prev_mu)
                ms1 = op("scalar", lambda e, pb=pb, c=c: e.activation(
                    out=dt_t[:], in_=pb[:, 0:LQ], func=AF.Exp,
                    bias=bdt_c[:, c:c + 1]), waits=s1w)
                psum_free[id(pb)] = ms1
                ms2 = op("scalar", lambda e: e.activation(
                    out=dt_t[:], in_=dt_t[:], func=AF.Ln, bias=csts[:, 1:2]),
                    waits=[ms1, m_cst])
                mu = op("vector", lambda e, c=c: e.tensor_tensor(
                    u_t[:], dt_t[:], hs[:, c * LQ:(c + 1) * LQ], ALU.mult),
                    waits=[ms2, conv_marks[c]])
                prev_mu = mu
                sb = c % 2
                gw = [spg_fin[0], spg_fin[1]]
                if sgc_free.get(sb) is not None:
                    gw.append(sgc_free[sb])
                msg = dma("gpsimd", sg_c[:, sb * LQ:(sb + 1) * LQ],
                          s_sg[c * 128:(c + 1) * 128, lo:lo + LQ], waits=gw,
                          sem=PSGR[sb])
                my = None
                W1 = LQ + 1
                a3 = a8[:].rearrange("p (g w) -> p g w", g=4)
                b3 = b8[:].rearrange("p (g w) -> p g w", g=4)
                h3 = h8[:].rearrange("p (g w) -> p g w", g=4)
                for g in range(4):
                    daw = [ms2, m_nA]
                    if prev_scan_g.get("last") is not None:
                        daw.append(prev_scan_g["last"])
                    mda = None
                    for j in range(4):
                        n = 4 * g + j
                        mda = op("scalar", lambda e, c=c, n=n, j=j: e.activation(
                            out=a8[:, j * W1 + 1:(j + 1) * W1], in_=dt_t[:],
                            func=AF.Exp,
                            scale=nA_c[:, c * N16 + n:c * N16 + n + 1]),
                            waits=daw if j == 0 else [])
                    mrs = op("vector", lambda e, c=c, g=g: e.tensor_copy(
                        b3[:, :, 0:1],
                        hstate[:, c * N16 + 4 * g:c * N16 + 4 * g + 4]
                        .unsqueeze(2)), waits=[m_hs0])
                    ub = u_t[:].unsqueeze(1).broadcast_to((128, 4, LQ))
                    mdb = op("vector", lambda e, g=g, ub=ub: e.tensor_tensor(
                        b3[:, :, 1:W1], ub, BBG(g), ALU.mult), waits=[mu, m_bb, mrs])
                    msc = op("vector", lambda e: e.tensor_tensor_scan(
                        h8[:], a8[:], b8[:], 0.0, ALU.mult, ALU.add),
                        waits=[mda, mdb, m_a80])
                    prev_scan_g["last"] = msc
                    op("vector", lambda e, c=c, g=g: e.tensor_copy(
                        hstate[:, c * N16 + 4 * g:c * N16 + 4 * g + 4]
                        .unsqueeze(2), h3[:, :, W1 - 1:W1]), waits=[msc])
                    ymw = [msc]
                    if g == 0 and prev_m2h is not None:
                        ymw.append(prev_m2h)
                    mym = op("vector", lambda e, g=g: e.tensor_tensor(
                        ym[:].rearrange("p (g l) -> p g l", g=4),
                        h3[:, :, 1:W1], CBG(g), ALU.mult), waits=ymw)
                    mt1 = op("vector", lambda e: e.tensor_tensor(
                        ym[:, 0:2 * LQ], ym[:, 0:2 * LQ], ym[:, 2 * LQ:4 * LQ],
                        ALU.add), waits=[mym])
                    if g == 0:
                        my = op("vector", lambda e: e.tensor_tensor(
                            y_t[:], ym[:, 0:LQ], ym[:, LQ:2 * LQ], ALU.add),
                            waits=[mt1])
                    else:
                        mt2 = op("vector", lambda e: e.tensor_tensor(
                            yt2[:], ym[:, 0:LQ], ym[:, LQ:2 * LQ], ALU.add),
                            waits=[mt1])
                        my = op("vector", lambda e: e.tensor_tensor(
                            y_t[:], y_t[:], yt2[:], ALU.add), waits=[mt2])
                my2 = op("vector", lambda e, c=c: e.scalar_tensor_tensor(
                    y_t[:], hs[:, c * LQ:(c + 1) * LQ], dvec_c[:, c:c + 1],
                    y_t[:], ALU.mult, ALU.add), waits=[my, m_dv])
                my3 = op("vector", lambda e, sb=sb: e.tensor_tensor(
                    yt2[:], y_t[:], sg_c[:, sb * LQ:(sb + 1) * LQ], ALU.mult),
                    waits=[my2, msg])
                sgc_free[sb] = my3
                h2w = [my3]
                if c == 0 and tapm.get("hs") is not None:
                    h2w.append(tapm["hs"])
                m2h = op("scalar", lambda e, c=c: e.copy(
                    HSB(c * 2 * LQ, c * 2 * LQ + LQ), yt2[:]), waits=h2w)
                prev_m2h = m2h
                m2l = op("vector", lambda e, c=c: e.tensor_tensor(
                    HSB(c * 2 * LQ + LQ, (c + 1) * 2 * LQ), yt2[:],
                    HSB(c * 2 * LQ, c * 2 * LQ + LQ), ALU.subtract),
                    waits=[m2h])
                y2_marks.append(m2l)
                scan_last = m2l

            if DEBUG and q == 0:
                tapm["y2"] = dma("gpsimd", taps["t_y2"][:, :], HSB(0, 16384),
                                 waits=[scan_last], sem=PTAP[0])
                tapm["hst"] = dma("gpsimd", taps["t_hst"][:, :], hstate[:],
                                  waits=[scan_last], sem=PTAP[0])

            ws_slot_c = {}
            mo_done = [[], []]
            m4_last_mm = None
            for hb in range(4):
                mm_last = {}
                for k in range(NKD):
                    slot = k % 8
                    so = slot * 2048
                    w = [y2_marks[k]]
                    if first_q and hb == 0 and k == 0:
                        w.extend(prep_last)
                    if hb == 0 and k < 8:
                        w.append(ws_slot[2 * (k % 8)])
                    if ws_slot_c.get(slot) is not None:
                        w.append(ws_slot_c[slot])
                    mh_l = dma("sync", WS[:, so:so + 512],
                               s_wout_h[k * 128:(k + 1) * 128,
                                        hb * 512:(hb + 1) * 512], waits=w,
                               sem=PWH[k % 16])
                    ml_l = dma("scalar", WS[:, so + 512:so + 1024],
                               s_wout_l[k * 128:(k + 1) * 128,
                                        hb * 512:(hb + 1) * 512], waits=w,
                               sem=PWL[k % 16])
                    for t in range(T2):
                        pb = ps[t]
                        pw = [mh_l, ml_l]
                        if k == 0 and psum_free.get(id(pb)) is not None:
                            pw.append(psum_free[id(pb)])
                        yh = HSB(k * 2 * LQ + t * 128, k * 2 * LQ + (t + 1) * 128)
                        yl = HSB(k * 2 * LQ + LQ + t * 128,
                                 k * 2 * LQ + LQ + (t + 1) * 128)
                        wh = WS[:, so:so + 512]
                        wl = WS[:, so + 512:so + 1024]
                        op("tensor", lambda e, pb=pb, yh=yh, wh=wh, s_=(k == 0):
                           e.matmul(pb[:, 0:512], yh, wh, start=s_, stop=False),
                           waits=pw if k == 0 else [mh_l, ml_l])
                        op("tensor", lambda e, pb=pb, yh=yh, wl=wl:
                           e.matmul(pb[:, 0:512], yh, wl, start=False,
                                    stop=False))
                        mm = op("tensor", lambda e, pb=pb, yl=yl, wh=wh,
                                p_=(k == NKD - 1):
                                e.matmul(pb[:, 0:512], yl, wh, start=False,
                                         stop=p_))
                        mm_last[t] = mm
                    ws_slot_c[slot] = mm_last[T2 - 1]
                m4_last_mm = mm_last[T2 - 1]
                for t in range(T2):
                    pb = ps[t]
                    ew = [mm_last[t]]
                    if hb == 0 and t == 0:
                        ew.append(m_x1[1])
                        ew.append(last_m1_mm)
                    mev = op("vector", lambda e, pb=pb, t=t, hb=hb: e.tensor_copy(
                        MO(t * H + hb * 512, t * H + (hb + 1) * 512),
                        pb[:, 0:512]), waits=ew)
                    psum_free[id(pb)] = mev
                    mo_done[t].append(mev)
            if DEBUG and q == 0:
                tapm["mo"] = dma("gpsimd", taps["t_mo"][:, :], MO(0, 4096),
                                 waits=mo_done[0] + mo_done[1], sem=PTAP[0])
            x2_done = []
            for t in range(T2):
                xs = MO(t * H, (t + 1) * H)
                ms = ln_stats(xs, 4, extra_waits=list(mo_done[t]))
                ma = op("vector", lambda e, xs=xs: e.tensor_scalar(
                    scr2k[:], xs, st[:, 4:5], st[:, 5:6], ALU.subtract,
                    ALU.mult), waits=[ms])
                mr_ = op("vector", lambda e, t=t: e.tensor_tensor(
                    x2[:, t * H:(t + 1) * H], scr2k[:],
                    xq[:, t * H:(t + 1) * H], ALU.add), waits=[ma, mx[t]])
                x2_done.append(mr_)
            xq_free = x2_done[-1]

            if DEBUG and q == 0:
                tapm["x2"] = dma("gpsimd", taps["t_x2"][:, :], x2[:],
                                 waits=x2_done, sem=PTAP[0])

            m_x3 = []
            prev_split = None
            for t in range(T2):
                xs = x2[:, t * H:(t + 1) * H]
                ms = ln_stats(xs, 8, extra_waits=[x2_done[t]])
                maw = [ms]
                if t == 0:
                    maw.append(x2_done[1])
                    if tapm.get("mo") is not None:
                        maw.append(tapm["mo"])
                if prev_split is not None:
                    maw.append(prev_split)
                ma = op("vector", lambda e, xs=xs: e.tensor_scalar(
                    scr2k[:], xs, st[:, 8:9], st[:, 9:10], ALU.subtract,
                    ALU.mult), waits=maw)
                prev_split = fm_split(t, [ma])
                m_x3.append(prev_split)

            gmax_init = [op("vector", lambda e, t=t: e.memset(
                st[:, 12 + t:13 + t], 0.0)) for t in range(T2)]
            gmax_last = [gmax_init[0], gmax_init[1]]
            ws_slot_d = {}
            sq_spills = {}
            m5_last_mm = None
            prev_mrx = None
            for fb in range(16):
                mm_last = {}
                for k in range(NKH):
                    so = k * 512
                    w = [m_x3[0], m_x3[1]]
                    if first_q and fb == 0 and k == 0:
                        pass
                    if fb == 0:
                        w.append(m4_last_mm)
                    if ws_slot_d.get(k) is not None:
                        w.append(ws_slot_d[k])
                    mq_l = dma("sync", WS[:, so:so + 512],
                               s_q1[k * 128:(k + 1) * 128,
                                    fb * 512:(fb + 1) * 512], waits=w,
                               sem=PWH[k])
                    for t in range(T2):
                        pb = ps[2 + t]
                        pw = [mq_l]
                        if k == 0 and psum_free.get(id(pb)) is not None:
                            pw.append(psum_free[id(pb)])
                        xh = x1f[:, k * LQ + t * 128: k * LQ + (t + 1) * 128]
                        xl = x1f[:, 4096 + k * LQ + t * 128:
                                 4096 + k * LQ + (t + 1) * 128]
                        wq = WS[:, so:so + 512]
                        op("tensor", lambda e, pb=pb, xh=xh, wq=wq, s_=(k == 0):
                           e.matmul(pb[:, 0:512], xh, wq, start=s_, stop=False),
                           waits=pw if k == 0 else [mq_l])
                        mm = op("tensor", lambda e, pb=pb, xl=xl, wq=wq,
                                p_=(k == NKH - 1):
                                e.matmul(pb[:, 0:512], xl, wq, start=False,
                                         stop=p_))
                        mm_last[t] = mm
                    ws_slot_d[k] = mm_last[T2 - 1]
                m5_last_mm = mm_last[T2 - 1]
                for t in range(T2):
                    pb = ps[2 + t]
                    rlw = [mm_last[t]]
                    if prev_mrx is not None:
                        rlw.append(prev_mrx)
                    mrl = op("scalar", lambda e, pb=pb: e.activation(
                        out=rg_t[:], in_=pb[:, 0:512], func=AF.Relu),
                        waits=rlw)
                    psum_free[id(pb)] = mrl
                    mrx = op("vector", lambda e: e.reduce_max(
                        st[:, 16:17], rg_t[:], axis=AXX), waits=[mrl])
                    prev_mrx = mrx
                    mgm = op("vector", lambda e, t=t: e.tensor_tensor(
                        st[:, 12 + t:13 + t], st[:, 12 + t:13 + t],
                        st[:, 16:17], ALU.max), waits=[mrx, gmax_last[t]])
                    gmax_last[t] = mgm
                    sb = (fb * 2 + t) % 2
                    sqw = [mrl, mrx]
                    if sq_free.get(sb) is not None:
                        sqw.append(sq_free[sb])
                    msq = op("scalar", lambda e, sb=sb: e.activation(
                        out=sq_t[:, sb * 512:(sb + 1) * 512], in_=rg_t[:],
                        func=AF.Square), waits=sqw)
                    sq_free[sb] = dma("gpsimd",
                                      s_s[t * 128:(t + 1) * 128,
                                          fb * 512:(fb + 1) * 512],
                                      sq_t[:, sb * 512:(sb + 1) * 512],
                                      waits=[msq], sem=PSPQ[sb])
                    sq_spills[(fb, t)] = sq_free[sb]

            spq_fin = (sq_free[0], sq_free[1])

            fso = []
            for t in range(T2):
                w1 = op("vector", lambda e, t=t: e.tensor_tensor(
                    st[:, 16 + t:17 + t], st[:, 12 + t:13 + t],
                    st[:, 12 + t:13 + t], ALU.mult), waits=[gmax_last[t]])
                w2 = op("vector", lambda e, t=t: e.tensor_scalar(
                    st[:, 16 + t:17 + t], st[:, 16 + t:17 + t],
                    sc_col[:, 2:3], EPS, ALU.mult, ALU.max),
                    waits=[w1, md2])
                w3 = op("vector", lambda e, t=t: e.reciprocal(
                    st[:, 18 + t:19 + t], st[:, 16 + t:17 + t]), waits=[w2])
                w4 = op("vector", lambda e, t=t: e.tensor_tensor(
                    st[:, 18 + t:19 + t], st[:, 18 + t:19 + t],
                    sc_col[:, 3:4], ALU.mult), waits=[w3])
                w5 = op("vector", lambda e, t=t: e.tensor_tensor(
                    st[:, 20 + t:21 + t], st[:, 16 + t:17 + t],
                    sc_col[:, 6:7], ALU.mult), waits=[w2, md3])
                fso.append((w4, w5))

            if DEBUG and q == 0:
                tapm["st"] = dma("gpsimd", taps["t_st"][:, :], st[:],
                                 waits=[fso[0][0], fso[0][1], fso[1][0],
                                        fso[1][1]], sem=PTAP[0])

            r_fm_marks = {}
            rr_last = None
            prev_rtr = None
            first_rfm = True
            for fb in range(16):
                for t in range(T2):
                    sb = (fb * 2 + t) % 2
                    rw = [spq_fin[0], spq_fin[1], fso[t][0]]
                    if sload_free.get(sb) is not None:
                        rw.append(sload_free[sb])
                    mld = dma("gpsimd", sload[:, sb * 512:(sb + 1) * 512],
                              s_s[t * 128:(t + 1) * 128,
                                  fb * 512:(fb + 1) * 512], waits=rw,
                              sem=PSLR[sb])
                    mv1 = op("vector", lambda e, sb=sb, t=t: e.tensor_scalar(
                        v16[:], sload[:, sb * 512:(sb + 1) * 512],
                        st[:, 18 + t:19 + t], 1024.0, ALU.mult, ALU.add),
                        waits=[mld])
                    sload_free[sb] = mv1
                    vw = [mv1]
                    if prev_rtr is not None:
                        vw.append(prev_rtr)
                    mv2 = op("vector", lambda e: e.tensor_scalar(
                        pj_t[:], v16[:], -1024.0, None, ALU.add), waits=vw)
                    for j in range(4):
                        rc = fb * 4 + j
                        pb2 = ps[6] if j % 2 == 0 else ps[7]
                        tw = [mv2]
                        if psum_free.get(id(pb2)) is not None:
                            tw.append(psum_free[id(pb2)])
                        mtr = op("tensor", lambda e, pb2=pb2, j=j: e.transpose(
                            pb2[:, 0:128], pj_t[:, j * 128:(j + 1) * 128],
                            eye_s[:]), waits=tw)
                        prev_rtr = mtr
                        hw = [mtr]
                        if first_rfm:
                            hw.append(m4_last_mm)
                            if tapm.get("y2") is not None:
                                hw.append(tapm["y2"])
                            first_rfm = False
                        mev = op("scalar", lambda e, pb2=pb2, rc=rc, t=t:
                                 e.activation(
                                     out=HSB(rc * LQ + t * 128,
                                             rc * LQ + (t + 1) * 128),
                                     in_=pb2[:, 0:128], func=AF.Copy),
                                 waits=hw)
                        psum_free[id(pb2)] = mev
                        r_fm_marks.setdefault(rc, []).append(mev)
                        rr_last = mev

            if DEBUG and q == 0:
                tapm["rfm"] = dma("gpsimd", taps["t_rfm"][:, :], HSB(0, 16384),
                                  waits=[rr_last], sem=PTAP[0])

            ws_slot_e = {}
            o6_done = [[], []]
            for db in range(4):
                mm_last = {}
                for k in range(NKF):
                    slot = k % 16
                    so = 8192 + slot * 512
                    w = []
                    if first_q and db == 0 and k == 0:
                        pass
                    if db == 0 and k < 16:
                        w.append(m4_last_mm)
                    if ws_slot_e.get(slot) is not None:
                        w.append(ws_slot_e[slot])
                    mq_l = dma("sync", WS[:, so:so + 512],
                               s_q2[k * 128:(k + 1) * 128,
                                    db * 512:(db + 1) * 512], waits=w,
                               sem=PM6[k % 16])
                    for t in range(T2):
                        pb = ps[t]
                        pw = [mq_l, r_fm_marks[k][-1]]
                        if k == 0 and psum_free.get(id(pb)) is not None:
                            pw.append(psum_free[id(pb)])
                        rf = HSB(k * LQ + t * 128, k * LQ + (t + 1) * 128)
                        wq = WS[:, so:so + 512]
                        mm = op("tensor", lambda e, pb=pb, rf=rf, wq=wq,
                                s_=(k == 0), p_=(k == NKF - 1):
                                e.matmul(pb[:, 0:512], rf, wq, start=s_,
                                         stop=p_),
                                waits=pw if (db == 0 or k == 0) else [mq_l])
                        mm_last[t] = mm
                    ws_slot_e[slot] = mm_last[T2 - 1]
                for t in range(T2):
                    pb = ps[t]
                    ew = [mm_last[t], fso[t][1]]
                    if db == 0:
                        ew.append(xq_free)
                    mev = op("vector", lambda e, pb=pb, t=t, db=db:
                             e.tensor_scalar(
                                 xq[:, t * H + db * 512:
                                    t * H + (db + 1) * 512],
                                 pb[:, 0:512], st[:, 20 + t:21 + t], None,
                                 ALU.mult), waits=ew)
                    psum_free[id(pb)] = mev
                    o6_done[t].append(mev)
            m6_last_mm = mm_last[T2 - 1]

            if DEBUG and q == 0:
                tapm["o6"] = dma("gpsimd", taps["t_o6"][:, :], xq[:],
                                 waits=o6_done[0] + o6_done[1], sem=PTAP[0])

            store_marks = []
            for t in range(T2):
                xs = xq[:, t * H:(t + 1) * H]
                lw = list(o6_done[t])
                if t == 1:
                    lw.append(store_marks[0])
                ms = ln_stats(xs, 24, extra_waits=lw)
                ma = op("vector", lambda e, xs=xs: e.tensor_scalar(
                    scr2k[:], xs, st[:, 24:25], st[:, 25:26], ALU.subtract,
                    ALU.mult), waits=[ms])
                mw_ = [ma]
                if prev_store is not None and t == 0:
                    mw_.append(prev_store)
                mr_ = op("vector", lambda e, t=t: e.tensor_tensor(
                    out_t[:, 0:H], scr2k[:], x2[:, t * H:(t + 1) * H],
                    ALU.add), waits=mw_)
                mst = dma("gpsimd", dout[lo + t * 128:lo + (t + 1) * 128, :],
                          out_t[:, 0:H], waits=[mr_], sem=PST[t])
                store_marks.append(mst)
            prev_store = store_marks[-1]
            out_dma_marks = list(store_marks)
            ws_free = m6_last_mm
            hs_free = m6_last_mm
            x1_free = m5_last_mm
            first_q = False

        prog.emit("vector", lambda e: e.engine_nop(),
                  waits=list(out_dma_marks))

        @block.sync
        def _(sync):
            prog.replay("sync", sync)

        @block.vector
        def _(vector):
            prog.replay("vector", vector)

        @block.scalar
        def _(scalar):
            prog.replay("scalar", scalar)

        @block.tensor
        def _(tensor):
            prog.replay("tensor", tensor)

        @block.gpsimd
        def _(gpsimd):
            prog.replay("gpsimd", gpsimd)

    return nc


def kernel(L0_attn_x, L0_attn_Win, L0_attn_Wconv, L0_attn_bconv, L0_attn_Wx,
           L0_attn_Wdt, L0_attn_bdt, L0_attn_Wout, L0_attn_A_log, L0_attn_D,
           L0_ffn_c_fc, L0_ffn_c_proj):
    x = np.ascontiguousarray(np.asarray(L0_attn_x, dtype=np.float32))
    winT = np.ascontiguousarray(np.asarray(L0_attn_Win, np.float32).T)
    wconv2 = np.ascontiguousarray(np.asarray(L0_attn_Wconv, np.float32)
                                  .reshape(NKD, 128, K4).transpose(1, 0, 2)
                                  .reshape(128, NKD * K4))
    bconv2 = np.ascontiguousarray(np.asarray(L0_attn_bconv, np.float32)
                                  .reshape(NKD, 128).T)
    wx2 = np.ascontiguousarray(np.asarray(L0_attn_Wx, np.float32).T
                               .reshape(NKD, 128, 160).transpose(1, 0, 2)
                               .reshape(128, NKD * 160))
    wdtT = np.ascontiguousarray(np.asarray(L0_attn_Wdt, np.float32).T)
    bdt2 = np.ascontiguousarray(np.asarray(L0_attn_bdt, np.float32)
                                .reshape(NKD, 128).T)
    woutT = np.ascontiguousarray(np.asarray(L0_attn_Wout, np.float32).T)
    alog2 = np.ascontiguousarray(np.asarray(L0_attn_A_log, np.float32)
                                 .reshape(NKD, 128, N16).transpose(1, 0, 2)
                                 .reshape(128, NKD * N16))
    dvec2 = np.ascontiguousarray(np.asarray(L0_attn_D, np.float32)
                                 .reshape(NKD, 128).T)
    cfc = np.ascontiguousarray(np.asarray(L0_ffn_c_fc, np.float32))
    cprojT = np.ascontiguousarray(np.asarray(L0_ffn_c_proj, np.float32).T)
    eye = np.eye(128, dtype=np.float32)

    nc = build_nc()
    in_maps = []
    for c in range(8):
        in_maps.append(dict(x=np.ascontiguousarray(x[c]), winT=winT,
                            wconv2=wconv2, bconv2=bconv2, wx2=wx2, wdtT=wdtT,
                            bdt2=bdt2, woutT=woutT, alog2=alog2, dvec2=dvec2,
                            cfc=cfc, cprojT=cprojT, eye=eye))
    res = run_bass_kernel_spmd(nc, in_maps, core_ids=list(range(8)))
    if DEBUG:
        kernel._last_res = res
    return np.stack([r["out"] for r in res.results], axis=0)
